# version 19
# baseline (speedup 1.0000x reference)
import math
from contextlib import ExitStack

import numpy as np
import ml_dtypes

import concourse.bass as bass
import concourse.mybir as mybir
from concourse.bass_utils import run_bass_kernel_spmd

F32 = mybir.dt.float32
F16 = mybir.dt.float16
BF16 = mybir.dt.bfloat16
U16 = mybir.dt.uint16
Alu = mybir.AluOpType
Act = mybir.ActivationFunctionType
AX = mybir.AxisListType.X

B, S, DM = 2, 4096, 2048
NCORE = 8
T = (B * S) // NCORE
I_FFN = 5632
NT_D = DM // 128
NT_I = I_FFN // 128
NH, DH = 16, 128
SCALE_QK = 1.0 / math.sqrt(DH)
SLOPES = [2.0 ** (-0.5 * (h + 1)) for h in range(NH)]
H_SCALE = 1.0 / 16.0
EPS_RMS = 1e-6 * H_SCALE * H_SCALE
EPS_LN = 1e-5
BIG = 60000.0
NKT = S // 128
NDELT = NKT + 3
GU_W = 22
LNHALF = math.log(0.5)
DW = NDELT * 128
_DBG = False
_STAGES = 7
_FFN_MODE = 3
_LN_MODE = 5
_FFN_SLICES = 2


def _perm(n):
    p = np.empty(n, dtype=np.int64)
    i = 0
    for c in range(n // 512):
        for r in range(4):
            for q in range(128):
                p[i] = 512 * c + 4 * q + r
                i += 1
    return p


PERM_D = _perm(DM)
PERM_I = _perm(I_FFN)


class Sem:
    def __init__(self, es, nc, name):
        self.h = es.enter_context(nc.semaphore(name))
        self.n = 0

    def inc(self, ins, k=1):
        ins.then_inc(self.h, k)
        self.n += k
        return self.n


class Ctx:
    pass


def build_program():
    nc = bass.Bass("TRN2", target_bir_lowering=False, detect_race_conditions=False)
    c = Ctx()
    c.nc = nc
    ap = lambda t: t.ap()
    c.xs = ap(nc.dram_tensor("xs", [DM, T], F32, kind="ExternalInput"))
    c.wqd = ap(nc.dram_tensor("wqd", [DM, DM], F32, kind="ExternalInput"))
    c.wkd = ap(nc.dram_tensor("wkd", [512, DM], F32, kind="ExternalInput"))
    c.wvd = ap(nc.dram_tensor("wvd", [512, DM], F32, kind="ExternalInput"))
    c.wod = ap(nc.dram_tensor("wod", [DM, DM], F32, kind="ExternalInput"))
    c.gupk = ap(nc.dram_tensor("gupk", [2 * I_FFN, DM // 4], U16, kind="ExternalInput"))
    c.gusc = ap(nc.dram_tensor("gusc", [DM // 128, 2 * I_FFN], F16, kind="ExternalInput"))
    c.dnpk = ap(nc.dram_tensor("dnpk", [DM, I_FFN // 4], U16, kind="ExternalInput"))
    c.dnsc = ap(nc.dram_tensor("dnsc", [I_FFN // 128, DM], F16, kind="ExternalInput"))
    c.delt = ap(nc.dram_tensor("delt", [2, DW, 128], F32, kind="ExternalInput"))
    c.bigm = ap(nc.dram_tensor("bigm", [2, DW, 128], F16, kind="ExternalInput"))
    c.dlt2 = ap(nc.dram_tensor("dlt2", [8 * 128, 128], F32, kind="ExternalInput"))
    c.msk2 = ap(nc.dram_tensor("msk2", [8 * 128, 128], F16, kind="ExternalInput"))
    c.ident = ap(nc.dram_tensor("ident", [128, 128], F16, kind="ExternalInput"))
    c.onecol_bf = ap(nc.dram_tensor("onecol_bf", [128, 1], BF16, kind="ExternalInput"))
    c.onecol_f16 = ap(nc.dram_tensor("onecol_f16", [128, 1], F16, kind="ExternalInput"))
    c.onerow_32 = ap(nc.dram_tensor("onerow_32", [1, 128], F32, kind="ExternalInput"))
    c.xo = ap(nc.dram_tensor("xo", [DM, T], F32, kind="ExternalOutput"))
    c.gu_scr = ap(nc.dram_tensor("gu_scr", [NT_D, 128, 2 * I_FFN], F16, kind="Internal"))
    c.dn_scr = ap(nc.dram_tensor("dn_scr", [NT_I, 128, DM], F16, kind="Internal"))
    c.xspill = ap(nc.dram_tensor("xspill", [128, NT_D * T], F32, kind="Internal"))
    c.cc_in = ap(nc.dram_tensor("cc_in", [2, 128, 4096], U16, kind="Internal"))
    c.cc_out = ap(nc.dram_tensor("cc_out", [4, 2, 128, 4096], U16, kind="Internal"))
    c.dpr_scr = ap(nc.dram_tensor("dpr_scr", [128, 2 * DW + 1024], F16, kind="Internal"))
    if _DBG:
        c.dbg_o = ap(nc.dram_tensor("dbg_o", [128, NH * 1024], F16, kind="ExternalOutput"))
        c.dbg_x2 = ap(nc.dram_tensor("dbg_x2", [128, NT_D * T], F32, kind="ExternalOutput"))
        c.dbg_ln2 = ap(nc.dram_tensor("dbg_ln2", [128, NT_D * 1024], F16, kind="ExternalOutput"))
        c.dbg_qt = ap(nc.dram_tensor("dbg_qt", [128, NH * 1024], F16, kind="ExternalOutput"))
        c.dbg_x1 = ap(nc.dram_tensor("dbg_x1", [128, NT_D * T], F32, kind="ExternalOutput"))
        c.dbg_rec = ap(nc.dram_tensor("dbg_rec", [2 * NH, 512], F32, kind="ExternalOutput"))
        c.dbg_nm = ap(nc.dram_tensor("dbg_nm", [128, 2 * NH], F32, kind="ExternalOutput"))

    with ExitStack() as es:
        E = es.enter_context
        for name in ["sIn0", "sIn1", "sSc", "sCc", "sPe", "sPs", "sPo", "sPr",
                     "sPw0", "sPw1", "sVe", "sVs", "sVr", "sVo", "sAc", "sAe",
                     "sAr", "sBar", "sWr0", "sWr1", "sVd", "sPp", "sAs", "sVh",
                     "sPy", "sAy", "sVq", "sPt", "sVt", "sPn", "sVn", "sVm"]:
            setattr(c, name, Sem(es, nc, name))

        c.ident_sb = E(nc.sbuf_tensor("ident_sb", [128, 128], F16))
        c.onecol_bf_sb = E(nc.sbuf_tensor("onecol_bf_sb", [128, 1], BF16))
        c.onecol_f16_sb = E(nc.sbuf_tensor("onecol_f16_sb", [128, 1], F16))
        c.onerow_32_sb = E(nc.sbuf_tensor("onerow_32_sb", [1, 128], F32))
        c.rowA = E(nc.sbuf_tensor("rowA", [1, 1024], F32))
        c.rowB = E(nc.sbuf_tensor("rowB", [1, 1024], F32))
        c.rowC = E(nc.sbuf_tensor("rowC", [1, 256], F32))
        c.rowD = E(nc.sbuf_tensor("rowD", [1, 256], F32))
        c.aB = E(nc.sbuf_tensor("aB", [128, 1024], F32))
        c.bB = E(nc.sbuf_tensor("bB", [128, 1024], F32))
        c.hsil = E(nc.sbuf_tensor("hsil", [128, 2 * 512], F16))
        c.tmpv = E(nc.sbuf_tensor("tmpv", [128, 512], F32))
        c.ysq = E(nc.sbuf_tensor("ysq", [128, 2 * 512], BF16))
        c.xb = E(nc.sbuf_tensor("xb", [128, 2 * 512], BF16))
        c.xsq = E(nc.sbuf_tensor("xsq", [128, 2 * 512], BF16))
        c.fence = E(nc.sbuf_tensor("fence", [1, 8], F32))
        c.ps = [E(nc.psum_tensor(f"ps{i}", [128, 512], F32)) for i in range(6)]
        c.pt = [E(nc.psum_tensor(f"pt{i}", [128, 1024], F16)) for i in range(2)]

        c.block = E(nc.Block())
        blk = c.block

        _emit_consts(c, blk)
        _emit_dprep(c, blk, nc)
        if _STAGES >= 2:
            _emit_dequant(c, blk, nc)

        if _STAGES >= 3:
            with nc.sbuf_tensor("xT1", [128, NT_D * T], F32, side="right") as xT:
                c.xT = xT
                _emit_load_x(c, blk)
                _emit_ffn(c, blk, nc, tag="A")

                def spill(scalar):
                    scalar.wait_ge(c.sVe.h, c.sVe.n)
                    c.sSc.inc(scalar.dma_start(out=c.xspill[:, :], in_=c.xT[:]), 16)
                    if _DBG:
                        c.sSc.inc(scalar.dma_start(out=c.dbg_x1[:, :], in_=c.xT[:]), 16)
                        scalar.wait_ge(c.sSc.h, c.sSc.n)
                blk.scalar(spill)
        if _STAGES < 4:
            _finish_partial(c, blk)
            return nc
        c.reg_o = ExitStack()
        c.reg_attn = ExitStack()
        c.reg_q = ExitStack()
        c.reg_ln2 = ExitStack()
        c.reg_kv = ExitStack()
        Ea = c.reg_attn.enter_context
        c.oT = c.reg_o.enter_context(nc.sbuf_tensor("oT", [128, NH * 1024], F16))
        c.dpr = Ea(nc.sbuf_tensor("dpr", [128, 2 * DW], F16))
        c.negM = Ea(nc.sbuf_tensor("negM", [128, 2 * NH], F32))
        c.smb = Ea(nc.sbuf_tensor("smb", [128, 4 * 512], F32))
        c.Eb = Ea(nc.sbuf_tensor("Eb", [128, 4 * 512], BF16))
        c.qsq = Ea(nc.sbuf_tensor("qsq", [128, 512], F16))
        c.xstg = Ea(nc.sbuf_tensor("xstg", [128, 2 * 1024], F32))
        c.QT = c.reg_q.enter_context(nc.sbuf_tensor("QT", [128, NH * 1024], F16))
        _emit_attn_pre(c, blk, nc)
        if _STAGES < 5:
            c.reg_kv.close(); c.reg_q.close(); c.reg_attn.close(); c.reg_o.close()
            _finish_partial(c, blk)
            return nc
        _emit_attn_main(c, blk)
        if _DBG:
            def dmp(sync):
                c.sIn0.inc(sync.dma_start(out=c.dbg_o[:, :], in_=c.oT[:]), 16)
                c.sIn0.inc(sync.dma_start(out=c.dbg_qt[:, :], in_=c.QT[:]), 16)
                c.sIn0.inc(sync.dma_start(out=c.dbg_nm[:, :], in_=c.negM[:]), 16)
                sync.wait_ge(c.sIn0.h, c.sIn0.n)
            blk.sync(dmp)
            _barrier(c, blk)
        c.reg_kv.close()
        c.reg_q.close()
        c.reg_attn.close()
        if _STAGES < 6:
            c.reg_o.close()
            _finish_partial(c, blk)
            return nc
        with nc.sbuf_tensor("xT2", [128, NT_D * T], F32, side="right") as xT2:
            c.xT = xT2
            _emit_reload_x(c, blk)
            _emit_wo(c, blk, nc)
            if _DBG:
                def dmp2(sync):
                    c.sIn0.inc(sync.dma_start(out=c.dbg_x2[:, :], in_=c.xT[:]), 16)
                    sync.wait_ge(c.sIn0.h, c.sIn0.n)
                blk.sync(dmp2)
                _barrier(c, blk)
            c.reg_o.close()
            if _STAGES >= 7:
                _emit_ffn(c, blk, nc, tag="B")
            _emit_store(c, blk)
    return nc


def _finish_partial(c, blk):
    _barrier(c, blk)

    def body(scalar):
        scalar.wait_ge(c.sBar.h, c.sBar.n)
    blk.scalar(body)


def _barrier(c, blk):
    c._bar = getattr(c, "_bar", 0) + 1
    tgt = 5 * c._bar

    def b_sync(eng):
        eng.nop().then_inc(c.sBar.h, 1)
        eng.wait_ge(c.sBar.h, tgt)

    def b_vec(eng):
        eng.tensor_copy(c.fence[0:1, 1:2], c.fence[0:1, 0:1]).then_inc(c.sBar.h, 1)
        eng.wait_ge(c.sBar.h, tgt)

    def b_sca(eng):
        eng.copy(c.fence[0:1, 3:4], c.fence[0:1, 2:3]).then_inc(c.sBar.h, 1)
        eng.wait_ge(c.sBar.h, tgt)

    def b_pe(eng):
        eng.ldweights(c.ident_sb[:, 0:1]).then_inc(c.sBar.h, 1)
        eng.wait_ge(c.sBar.h, tgt)

    def b_gp(eng):
        eng.tensor_copy(c.fence[0:1, 5:6], c.fence[0:1, 4:5]).then_inc(c.sBar.h, 1)
        eng.wait_ge(c.sBar.h, tgt)

    blk.sync(b_sync)
    blk.vector(b_vec)
    blk.scalar(b_sca)
    blk.tensor(b_pe)
    blk.gpsimd(b_gp)
    c.sBar.n += 5


def _emit_consts(c, blk):
    def body(sync):
        for dst, src in [(c.ident_sb, c.ident), (c.onecol_bf_sb, c.onecol_bf),
                         (c.onecol_f16_sb, c.onecol_f16), (c.onerow_32_sb, c.onerow_32)]:
            c.sIn0.inc(sync.dma_start(out=dst[:], in_=src[:, :]), 16)
    blk.sync(body)
    tin = c.sIn0.n

    def vinit(vector):
        vector.memset(c.fence[0:1, 0:2], 0.0)
    blk.vector(vinit)

    def sinit(scalar):
        scalar.memzero(c.fence[0:1, 2:4])
    blk.scalar(sinit)

    def ginit(gpsimd):
        gpsimd.memset(c.fence[0:1, 4:6], 0.0)
    blk.gpsimd(ginit)

    def peinit(tensor, tin=tin):
        tensor.wait_ge(c.sIn0.h, tin)
    blk.tensor(peinit)


def _emit_dprep(c, blk, nc):
    with nc.sbuf_tensor("dta", [128, DW], F32) as dta, \
         nc.sbuf_tensor("dtb", [128, DW], F32) as dtb, \
         nc.sbuf_tensor("dtd", [128, DW], F16) as dtd, \
         nc.sbuf_tensor("dtc", [128, DW], F16) as dtc:
        for ch in range(2):
            def ld(sync, ch=ch):
                sync.wait_ge(c.sVe.h, c.sVe.n)
                c.sIn0.inc(sync.dma_start(
                    out=dta.ap()[:, :].rearrange("p (i f) -> p i f", f=128),
                    in_=c.delt[ch].rearrange("(i p) f -> p i f", p=128)), 16)
                c.sIn0.inc(sync.dma_start(
                    out=dtc.ap()[:, :].rearrange("p (i f) -> p i f", f=128),
                    in_=c.bigm[ch].rearrange("(i p) f -> p i f", p=128)), 16)
            blk.sync(ld)
            tin = c.sIn0.n

            def sc(scalar, tin=tin):
                scalar.wait_ge(c.sIn0.h, tin)
                scalar.activation(dtb[:], dta[:], Act.Ln)
                c.sAc.inc(scalar.activation(dta[:], dtb[:], Act.Exp, scale=0.5))
            blk.scalar(sc)
            ta = c.sAc.n

            def ve(vector, ch=ch, ta=ta):
                vector.wait_ge(c.sAc.h, ta)
                vector.wait_ge(c.sWr0.h, c.sWr0.n)
                c.sVe.inc(vector.tensor_tensor(dtd[:], dta[:], dtc[:], Alu.add))
            blk.vector(ve)
            tvv = c.sVe.n

            def wr(sync, ch=ch, tvv=tvv):
                sync.wait_ge(c.sVe.h, tvv)
                c.sWr0.inc(sync.dma_start(
                    out=c.dpr_scr[:, ch * DW:(ch + 1) * DW], in_=dtd[:]), 16)
            blk.sync(wr)

        def ld2(sync):
            sync.wait_ge(c.sVe.h, c.sVe.n)
            c.sIn0.inc(sync.dma_start(
                out=dta[:, 0:1024].rearrange("p (i f) -> p i f", f=128),
                in_=c.dlt2.rearrange("(i p) f -> p i f", p=128)), 16)
            c.sIn0.inc(sync.dma_start(
                out=dtc[:, 0:1024].rearrange("p (i f) -> p i f", f=128),
                in_=c.msk2.rearrange("(i p) f -> p i f", p=128)), 16)
        blk.sync(ld2)
        tin2 = c.sIn0.n

        def sc2(scalar, tin2=tin2):
            scalar.wait_ge(c.sIn0.h, tin2)
            scalar.activation(dtb[:, 0:1024], dta[:, 0:1024], Act.Ln)
            c.sAc.inc(scalar.activation(dta[:, 0:1024], dtb[:, 0:1024], Act.Exp, scale=0.5))
        blk.scalar(sc2)
        ta2x = c.sAc.n

        def ve2(vector, ta2x=ta2x):
            vector.wait_ge(c.sAc.h, ta2x)
            c.sVe.inc(vector.tensor_tensor(dtd[:, 0:1024], dta[:, 0:1024],
                                           dtc[:, 0:1024], Alu.add))
        blk.vector(ve2)
        tvv2 = c.sVe.n

        def wr2(sync, tvv2=tvv2):
            sync.wait_ge(c.sVe.h, tvv2)
            c.sWr0.inc(sync.dma_start(
                out=c.dpr_scr[:, 2 * DW:2 * DW + 1024], in_=dtd[:, 0:1024]), 16)
        blk.sync(wr2)

        def drain(sync):
            sync.wait_ge(c.sWr0.h, c.sWr0.n)
        blk.sync(drain)
    _barrier(c, blk)


def _emit_dequant(c, blk, nc):
    W2 = 2 * I_FFN
    HW = I_FFN
    slices = [("gu", sl) for sl in range(4)] + [("dn", sl) for sl in range(11)]
    with nc.sbuf_tensor("dq_u", [128, 2 * W2], U16) as u, \
         nc.sbuf_tensor("dq_s", [128, 2 * W2], F16) as scl, \
         nc.sbuf_tensor("dq_t1", [128, HW], U16) as t1, \
         nc.sbuf_tensor("dq_t2", [128, HW], U16) as t2, \
         nc.sbuf_tensor("dq_pl", [128, 2 * W2], F16) as pl:
        base_sc = c.sSc.n
        base_vd = c.sVd.n
        base_wr = c.sWr0.n

        def ld(scalar):
            for t, (kind, sl) in enumerate(slices):
                W = W2 if kind == "gu" else DM
                pk = c.gupk if kind == "gu" else c.dnpk
                sc = c.gusc if kind == "gu" else c.dnsc
                par = t % 2
                if t >= 2:
                    scalar.wait_ge(c.sVd.h, base_vd + 4 * (t - 1))
                c.sSc.inc(scalar.dma_start_transpose(
                    u[:, par * W2: par * W2 + W], pk[:, 128 * sl:128 * (sl + 1)]), 16)
                for g in range(4):
                    c.sSc.inc(scalar.dma_start(
                        out=scl[32 * g:32 * (g + 1), par * W2: par * W2 + W],
                        in_=sc[4 * sl + g:4 * sl + g + 1, :].broadcast_to([32, W])), 16)
        blk.scalar(ld)

        def dq(vector):
            for t, (kind, sl) in enumerate(slices):
                W = W2 if kind == "gu" else DM
                par = t % 2
                vector.wait_ge(c.sSc.h, base_sc + 80 * (t + 1))
                for r in range(4):
                    pidx = 4 * t + r
                    ppar = pidx % 2
                    if pidx >= 2:
                        vector.wait_ge(c.sWr0.h, base_wr + 16 * (pidx - 1))
                    shs, shm = 12 - 4 * r, 9 - 4 * r
                    ops = Alu.logical_shift_left if shs >= 0 else Alu.logical_shift_right
                    opm = Alu.logical_shift_left if shm >= 0 else Alu.logical_shift_right
                    nh2 = (W + HW - 1) // HW
                    for hh in range(nh2):
                        w0 = hh * HW
                        w1 = min(W, w0 + HW)
                        n = w1 - w0
                        usl = u[:, par * W2 + w0: par * W2 + w1]
                        ssl = scl[:, par * W2 + w0: par * W2 + w1]
                        vector.tensor_scalar(t1[:, 0:n], usl, int(abs(shs)),
                                             int(0x8000), ops, Alu.bitwise_and)
                        vector.tensor_scalar(t2[:, 0:n], usl, int(abs(shm)),
                                             int(0x0E00), opm, Alu.bitwise_and)
                        vector.tensor_tensor(t1[:, 0:n], t1[:, 0:n], t2[:, 0:n],
                                             Alu.bitwise_or)
                        ins = vector.scalar_tensor_tensor(
                            pl[:, ppar * W2 + w0: ppar * W2 + w1],
                            t1.ap()[:, 0:n].bitcast(F16), 16384.0, ssl,
                            Alu.mult, Alu.mult)
                    c.sVd.inc(ins)
        blk.vector(dq)

        def wr(sync):
            for t, (kind, sl) in enumerate(slices):
                W = W2 if kind == "gu" else DM
                for r in range(4):
                    pidx = 4 * t + r
                    ppar = pidx % 2
                    kap = 4 * sl + r
                    sync.wait_ge(c.sVd.h, base_vd + pidx + 1)
                    dst = c.gu_scr[kap] if kind == "gu" else c.dn_scr[kap]
                    c.sWr0.inc(sync.dma_start(
                        out=dst, in_=pl[:, ppar * W2: ppar * W2 + W]), 16)
            sync.wait_ge(c.sWr0.h, c.sWr0.n)
        blk.sync(wr)
    _barrier(c, blk)


def _emit_load_x(c, blk):
    def body(sync):
        c.sIn0.inc(sync.dma_start(
            out=c.xT.ap()[:, :].rearrange("p (tau t) -> p tau t", t=T),
            in_=c.xs.rearrange("(tau p) t -> p tau t", p=128)), 16)
    blk.sync(body)
    tin = c.sIn0.n

    def gate(vector):
        vector.wait_ge(c.sIn0.h, tin)
    blk.vector(gate)


def _emit_reload_x(c, blk):
    def body(scalar):
        c.sSc.inc(scalar.dma_start(out=c.xT[:], in_=c.xspill[:, :]), 16)
    blk.scalar(body)
    tin = c.sSc.n

    def gate(vector):
        vector.wait_ge(c.sSc.h, tin)
    blk.vector(gate)


def _emit_store(c, blk):
    _barrier(c, blk)

    def body(scalar):
        c.sSc.inc(scalar.dma_start(
            out=c.xo.rearrange("(tau p) t -> p tau t", p=128),
            in_=c.xT.ap()[:, :].rearrange("p (tau t) -> p tau t", t=T)), 16)
        scalar.wait_ge(c.sSc.h, c.sSc.n)
    blk.scalar(body)


def _emit_ln(c, blk, col0, W, dest):
    nh = W // 512
    base_spe = c.sPe.n
    base_sve = c.sVe.n

    def vcast(vector):
        for i in range(NT_D * nh):
            tau, h = i // nh, i % nh
            sl2 = 512 * (i % 2)
            if i >= 2:
                vector.wait_ge(c.sPe.h, base_spe + i - 1)
            sl = slice(T * tau + col0 + 512 * h, T * tau + col0 + 512 * (h + 1))
            vector.tensor_copy(c.xb[:, sl2:sl2 + 512], c.xT[:, sl])
            c.sVe.inc(vector.tensor_tensor(
                c.xsq[:, sl2:sl2 + 512], c.xb[:, sl2:sl2 + 512],
                c.xb[:, sl2:sl2 + 512], Alu.mult))
    blk.vector(vcast)

    t_ay = c.sAy.n
    t_acx = c.sAc.n

    def pstat(tensor):
        tensor.wait_ge(c.sAy.h, t_ay)
        tensor.wait_ge(c.sAc.h, t_acx)
        for i in range(NT_D * nh):
            tau, h = i // nh, i % nh
            sl2 = 512 * (i % 2)
            tensor.wait_ge(c.sVe.h, base_sve + i + 1)
            tensor.matmul(c.ps[2 + h][0:1, :], c.onecol_bf_sb[:],
                          c.xb[:, sl2:sl2 + 512],
                          start=(tau == 0), stop=(tau == NT_D - 1))
            c.sPe.inc(tensor.matmul(
                c.ps[4 + h][0:1, :], c.onecol_bf_sb[:], c.xsq[:, sl2:sl2 + 512],
                start=(tau == 0), stop=(tau == NT_D - 1)))
    blk.tensor(pstat)
    tpe = c.sPe.n
    if _LN_MODE < 2:
        return

    def vrows(vector):
        vector.wait_ge(c.sPe.h, tpe)
        for h in range(nh):
            sl = slice(512 * h, 512 * (h + 1))
            vector.tensor_scalar(c.rowA[0:1, sl], c.ps[2 + h][0:1, :], 1.0 / DM,
                                 None, Alu.mult)
            vector.tensor_scalar(c.rowB[0:1, sl], c.ps[4 + h][0:1, :], 1.0 / DM,
                                 None, Alu.mult)
        vector.tensor_tensor(c.aB[0:1, 0:W], c.rowA[0:1, 0:W], c.rowA[0:1, 0:W], Alu.mult)
        vector.tensor_tensor(c.rowB[0:1, 0:W], c.rowB[0:1, 0:W], c.aB[0:1, 0:W], Alu.subtract)
        vector.tensor_scalar(c.rowB[0:1, 0:W], c.rowB[0:1, 0:W], EPS_LN, None, Alu.add)
        c.sVe.inc(vector.tensor_scalar(
            c.rowA[0:1, 0:W], c.rowA[0:1, 0:W], -1.0, None, Alu.mult))
    blk.vector(vrows)
    tv = c.sVe.n

    def srows(scalar, tv=tv):
        scalar.wait_ge(c.sVe.h, tv)
        scalar.activation(c.bB[0:1, 0:W], c.rowB[0:1, 0:W], Act.Ln)
        c.sAc.inc(scalar.activation(c.rowB[0:1, 0:W], c.bB[0:1, 0:W], Act.Exp,
                                    scale=-0.5))
    blk.scalar(srows)
    ta = c.sAc.n
    if _LN_MODE < 3:
        return

    def pbc(tensor, ta=ta):
        tensor.wait_ge(c.sAc.h, ta)
        for h in range(nh):
            sl = slice(512 * h, 512 * (h + 1))
            tensor.matmul(c.ps[2 + h][:, :], c.onerow_32_sb[:], c.rowA[0:1, sl],
                          start=True, stop=True)
            ins = tensor.matmul(c.ps[4 + h][:, :], c.onerow_32_sb[:], c.rowB[0:1, sl],
                                start=True, stop=True)
        c.sPe.inc(ins)
    blk.tensor(pbc)
    tpe2 = c.sPe.n
    if _LN_MODE < 4:
        return

    def sevac(vector, tpe2=tpe2):
        vector.wait_ge(c.sPe.h, tpe2)
        for h in range(nh):
            sl = slice(512 * h, 512 * (h + 1))
            vector.tensor_copy(c.bB[:, sl], c.ps[2 + h][:, :])
            ins = vector.tensor_copy(c.aB[:, sl], c.ps[4 + h][:, :])
        c.sAc.inc(ins)
    blk.vector(sevac)
    ta2 = c.sAc.n
    if _LN_MODE < 5:
        return

    def vapply(vector, ta2=ta2):
        vector.wait_ge(c.sAc.h, ta2)
        for tau in range(NT_D):
            xsl = c.xT[:, T * tau + col0: T * tau + col0 + W]
            vector.tensor_tensor(dest[:, tau, 0:W], xsl, c.bB[:, 0:W], Alu.add)
            ins = vector.tensor_tensor(dest[:, tau, 0:W], dest[:, tau, 0:W],
                                       c.aB[:, 0:W], Alu.mult)
        c.sVe.inc(ins)
    blk.vector(vapply)



def _emit_ln2_hbm(c, blk, dest):
    tspill = c.sSc.n
    base_i = c.sIn0.n
    base_vn = c.sVn.n

    def dmas(sync):
        sync.wait_ge(c.sSc.h, tspill)
        for tau in range(NT_D):
            par = tau % 2
            if tau >= 2:
                sync.wait_ge(c.sVn.h, base_vn + 2 * (tau - 1))
            c.sIn0.inc(sync.dma_start(out=c.xstg[:, 1024 * par:1024 * (par + 1)],
                                      in_=c.xspill[:, T * tau:T * (tau + 1)]), 16)
    blk.sync(dmas)

    base_spe2 = c.sPe.n

    def vcast(vector):
        for i in range(NT_D * 2):
            tau, h = i // 2, i % 2
            par = tau % 2
            sl2 = 512 * (i % 2)
            vector.wait_ge(c.sIn0.h, base_i + 16 * (tau + 1))
            if i >= 2:
                vector.wait_ge(c.sPe.h, base_spe2 + i - 1)
            sl = c.xstg[:, 1024 * par + 512 * h: 1024 * par + 512 * (h + 1)]
            vector.tensor_copy(c.xb[:, sl2:sl2 + 512], sl)
            c.sVn.inc(vector.tensor_tensor(c.xsq[:, sl2:sl2 + 512],
                                           c.xb[:, sl2:sl2 + 512],
                                           c.xb[:, sl2:sl2 + 512], Alu.mult))
    blk.vector(vcast)

    def pstat(tensor):
        for i in range(NT_D * 2):
            tau, h = i // 2, i % 2
            sl2 = 512 * (i % 2)
            tensor.wait_ge(c.sVn.h, base_vn + i + 1)
            tensor.matmul(c.ps[2 + h][0:1, :], c.onecol_bf_sb[:],
                          c.xb[:, sl2:sl2 + 512],
                          start=(tau == 0), stop=(tau == NT_D - 1))
            c.sPe.inc(tensor.matmul(
                c.ps[4 + h][0:1, :], c.onecol_bf_sb[:], c.xsq[:, sl2:sl2 + 512],
                start=(tau == 0), stop=(tau == NT_D - 1)))
    blk.tensor(pstat)
    tpe = c.sPe.n
    W = 1024

    def vrows(vector):
        vector.wait_ge(c.sPe.h, tpe)
        for h in range(2):
            sl = slice(512 * h, 512 * (h + 1))
            vector.tensor_scalar(c.rowA[0:1, sl], c.ps[2 + h][0:1, :], 1.0 / DM,
                                 None, Alu.mult)
            vector.tensor_scalar(c.rowB[0:1, sl], c.ps[4 + h][0:1, :], 1.0 / DM,
                                 None, Alu.mult)
        vector.tensor_tensor(c.aB[0:1, 0:W], c.rowA[0:1, 0:W], c.rowA[0:1, 0:W], Alu.mult)
        vector.tensor_tensor(c.rowB[0:1, 0:W], c.rowB[0:1, 0:W], c.aB[0:1, 0:W], Alu.subtract)
        vector.tensor_scalar(c.rowB[0:1, 0:W], c.rowB[0:1, 0:W], EPS_LN, None, Alu.add)
        c.sVe.inc(vector.tensor_scalar(
            c.rowA[0:1, 0:W], c.rowA[0:1, 0:W], -1.0, None, Alu.mult))
    blk.vector(vrows)
    tv = c.sVe.n

    def srows(scalar, tv=tv):
        scalar.wait_ge(c.sVe.h, tv)
        scalar.activation(c.bB[0:1, 0:W], c.rowB[0:1, 0:W], Act.Ln)
        c.sAc.inc(scalar.activation(c.rowB[0:1, 0:W], c.bB[0:1, 0:W], Act.Exp,
                                    scale=-0.5))
    blk.scalar(srows)
    ta = c.sAc.n

    def pbc(tensor, ta=ta):
        tensor.wait_ge(c.sAc.h, ta)
        for h in range(2):
            sl = slice(512 * h, 512 * (h + 1))
            tensor.matmul(c.ps[2 + h][:, :], c.onerow_32_sb[:], c.rowA[0:1, sl],
                          start=True, stop=True)
            ins = tensor.matmul(c.ps[4 + h][:, :], c.onerow_32_sb[:], c.rowB[0:1, sl],
                                start=True, stop=True)
        c.sPe.inc(ins)
    blk.tensor(pbc)
    tpe2 = c.sPe.n

    def sevac(scalar, tpe2=tpe2):
        scalar.wait_ge(c.sPe.h, tpe2)
        for h in range(2):
            sl = slice(512 * h, 512 * (h + 1))
            scalar.copy(c.bB[:, sl], c.ps[2 + h][:, :])
            ins = scalar.copy(c.aB[:, sl], c.ps[4 + h][:, :])
        c.sAc.inc(ins)
    blk.scalar(sevac)
    ta2 = c.sAc.n

    base_i2 = c.sIn0.n
    base_va = c.sVn.n

    def dmas2(sync):
        sync.wait_ge(c.sVn.h, base_vn + 2 * NT_D)
        for tau in range(NT_D):
            par = tau % 2
            if tau >= 2:
                sync.wait_ge(c.sVn.h, base_va + (tau - 1))
            c.sIn0.inc(sync.dma_start(out=c.xstg[:, 1024 * par:1024 * (par + 1)],
                                      in_=c.xspill[:, T * tau:T * (tau + 1)]), 16)
    blk.sync(dmas2)

    def vapply(vector):
        vector.wait_ge(c.sAc.h, ta2)
        for tau in range(NT_D):
            par = tau % 2
            vector.wait_ge(c.sIn0.h, base_i2 + 16 * (tau + 1))
            xsl = c.xstg[:, 1024 * par:1024 * (par + 1)]
            vector.tensor_tensor(dest[:, tau, 0:1024], xsl, c.bB[:, 0:1024], Alu.add)
            c.sVn.inc(vector.tensor_tensor(dest[:, tau, 0:1024], dest[:, tau, 0:1024],
                                           c.aB[:, 0:1024], Alu.mult))
    blk.vector(vapply)


def _emit_ffn(c, blk, nc, tag):
    with nc.sbuf_tensor(f"hT{tag}", [128, NT_I, 512], F16) as hT, \
         nc.sbuf_tensor(f"wbf{tag}", [128, 2 * 16 * 512], F16) as wbf, \
         nc.sbuf_tensor(f"ub{tag}", [128, NT_D * 512], F32) as ubuf:
        lnb = ubuf.ap()[:, :].bitcast(F16)[:, 0:NT_D * 512].rearrange(
            "p (k f) -> p k f", f=512)
        yb = ubuf.ap()[:, :].rearrange("p (k f) -> p k f", f=512)
        wb = [wbf.ap()[:, 0:8192].rearrange("p (k f) -> p k f", k=16),
              wbf.ap()[:, 8192:16384].rearrange("p (k f) -> p k f", k=16)]
        dnb = [wbf.ap()[:, 0:NT_I * 128].rearrange("p (k f) -> p k f", k=NT_I),
               wbf.ap()[:, 8192:8192 + NT_I * 128].rearrange("p (k f) -> p k f", k=NT_I)]
        for sl in range(_FFN_SLICES):
            col0 = 512 * sl
            if _FFN_MODE >= 1:
                _emit_ln(c, blk, col0, 512, lnb)
            if _FFN_MODE >= 2:
                _emit_ffn1(c, blk, wb, lnb, hT)
            if _FFN_MODE >= 3:
                _emit_ffn2(c, blk, dnb, hT, yb, col0)
        _barrier(c, blk)


def _emit_ffn1(c, blk, wb, lnb, hT):
    base_i = [c.sIn0.n, c.sIn1.n]
    base_pp, base_as, base_vh = c.sPp.n, c.sAs.n, c.sVh.n
    t_lnb = c.sVe.n
    t_sev = c.sAc.n
    t_pyp = c.sPy.n

    def dmas(sync):
        sync.wait_ge(c.sPy.h, t_pyp)
        for w in range(GU_W):
            par = w % 2
            sem = [c.sIn0, c.sIn1][par]
            if w >= 2:
                sync.wait_ge(c.sPp.h, base_pp + 2 * (w - 1))
            sem.inc(sync.dma_start(out=wb[par][:, :, :],
                                   in_=c.gu_scr[:, :, 512 * w:512 * (w + 1)]
                                   .rearrange("k p f -> p k f")), 16)
    blk.sync(dmas)

    def mms(tensor):
        tensor.wait_ge(c.sVe.h, t_lnb)
        tensor.wait_ge(c.sAc.h, t_sev)
        for w in range(GU_W):
            par = w % 2
            sem = [c.sIn0, c.sIn1][par]
            tensor.wait_ge(sem.h, base_i[par] + 16 * (w // 2 + 1))
            for i in range(4):
                p = 2 * w + i // 2
                bankset = p % 2
                dst = c.ps[0 + bankset] if i % 2 == 0 else c.ps[2 + bankset]
                if i % 2 == 0 and p >= 2:
                    tensor.wait_ge(c.sAs.h, base_as + p - 1)
                    tensor.wait_ge(c.sVh.h, base_vh + p - 1)
                for k in range(NT_D):
                    ins = tensor.matmul(dst[:, :], wb[par][:, k, 128 * i:128 * (i + 1)],
                                        lnb[:, k, :], start=(k == 0), stop=(k == NT_D - 1))
                if i % 2 == 1:
                    c.sPp.inc(ins)
    blk.tensor(mms)

    def silu_s(scalar):
        for p in range(2 * GU_W):
            bankset = p % 2
            scalar.wait_ge(c.sPp.h, base_pp + p + 1)
            if p >= 2:
                scalar.wait_ge(c.sVh.h, base_vh + p - 1)
            c.sAs.inc(scalar.activation(c.hsil[:, 512 * bankset:512 * (bankset + 1)],
                                        c.ps[0 + bankset][:, :], Act.Silu))
    blk.scalar(silu_s)

    def hmul(vector):
        for p in range(2 * GU_W):
            bankset = p % 2
            vector.wait_ge(c.sAs.h, base_as + p + 1)
            c.sVh.inc(vector.scalar_tensor_tensor(
                hT[:, p, :], c.ps[2 + bankset][:, :], H_SCALE,
                c.hsil[:, 512 * bankset:512 * (bankset + 1)], Alu.mult, Alu.mult))
    blk.vector(hmul)


def _emit_ffn2(c, blk, dnb, hT, yb, col0):
    base_i = [c.sIn0.n, c.sIn1.n]
    base_py, base_ay, base_vq = c.sPy.n, c.sAy.n, c.sVq.n
    base_pe = c.sPe.n
    t_pp = c.sPp.n
    t_vh = c.sVh.n

    def dmas(sync):
        sync.wait_ge(c.sPp.h, t_pp)
        for m in range(NT_D):
            par = m % 2
            sem = [c.sIn0, c.sIn1][par]
            if m >= 2:
                sync.wait_ge(c.sPy.h, base_py + m - 1)
            sem.inc(sync.dma_start(out=dnb[par][:, :, :],
                                   in_=c.dn_scr[:, :, 128 * m:128 * (m + 1)]
                                   .rearrange("k p f -> p k f")), 16)
    blk.sync(dmas)

    def mms(tensor):
        tensor.wait_ge(c.sVh.h, t_vh)
        for m in range(NT_D):
            par = m % 2
            sem = [c.sIn0, c.sIn1][par]
            tensor.wait_ge(sem.h, base_i[par] + 16 * (m // 2 + 1))
            if m >= 2:
                tensor.wait_ge(c.sAy.h, base_ay + m - 1)
            dst = c.ps[4 + par]
            for k in range(NT_I):
                ins = tensor.matmul(dst[:, :], dnb[par][:, k, :], hT[:, k, :],
                                    start=(k == 0), stop=(k == NT_I - 1))
            c.sPy.inc(ins)
    blk.tensor(mms)

    def yevac(scalar):
        for m in range(NT_D):
            par = m % 2
            scalar.wait_ge(c.sPy.h, base_py + m + 1)
            c.sAy.inc(scalar.copy(yb[:, m, :], c.ps[4 + par][:, :]))
    blk.scalar(yevac)

    def vsq(vector):
        for m in range(NT_D):
            vector.wait_ge(c.sAy.h, base_ay + m + 1)
            if m >= 2:
                vector.wait_ge(c.sPe.h, base_pe + m - 1)
            c.sVq.inc(vector.tensor_tensor(
                c.ysq[:, 512 * (m % 2):512 * (m % 2 + 1)], yb[:, m, :],
                yb[:, m, :], Alu.mult))
    blk.vector(vsq)

    def pstat(tensor):
        for m in range(NT_D):
            tensor.wait_ge(c.sVq.h, base_vq + m + 1)
            c.sPe.inc(tensor.matmul(
                c.ps[2][0:1, :], c.onecol_bf_sb[:],
                c.ysq[:, 512 * (m % 2):512 * (m % 2 + 1)],
                start=(m == 0), stop=(m == NT_D - 1)))
    blk.tensor(pstat)
    tstat = c.sPe.n

    def vrow(vector, tstat=tstat):
        vector.wait_ge(c.sPe.h, tstat)
        c.sVe.inc(vector.tensor_scalar(c.rowA[0:1, 0:512], c.ps[2][0:1, :],
                                       1.0 / DM, EPS_RMS, Alu.mult, Alu.add))
    blk.vector(vrow)
    tv = c.sVe.n

    def srow(scalar, tv=tv):
        scalar.wait_ge(c.sVe.h, tv)
        scalar.activation(c.rowB[0:1, 0:512], c.rowA[0:1, 0:512], Act.Ln)
        scalar.activation(c.rowA[0:1, 0:512], c.rowB[0:1, 0:512], Act.Exp, scale=-0.5)
        c.sAc.inc(scalar.mul(c.rowA[0:1, 0:512], c.rowA[0:1, 0:512], 0.5))
    blk.scalar(srow)
    ta = c.sAc.n

    def pbc(tensor, ta=ta):
        tensor.wait_ge(c.sAc.h, ta)
        c.sPe.inc(tensor.matmul(c.ps[3][:, :], c.onerow_32_sb[:], c.rowA[0:1, 0:512],
                                start=True, stop=True))
    blk.tensor(pbc)
    tb = c.sPe.n

    def sevac(scalar, tb=tb):
        scalar.wait_ge(c.sPe.h, tb)
        c.sAc.inc(scalar.copy(c.aB[:, 0:512], c.ps[3][:, :]))
    blk.scalar(sevac)
    ta2 = c.sAc.n

    def vadd(vector, ta2=ta2):
        vector.wait_ge(c.sAc.h, ta2)
        for m in range(NT_D):
            vector.tensor_tensor(c.tmpv[:, 0:512], yb[:, m, :], c.aB[:, 0:512], Alu.mult)
            xsl = c.xT[:, T * m + col0: T * m + col0 + 512]
            ins = vector.tensor_tensor(xsl, xsl, c.tmpv[:, 0:512], Alu.add)
        c.sVe.inc(ins)
    blk.vector(vadd)


def _emit_attn_pre(c, blk, nc):
    ln2 = c.reg_ln2.enter_context(nc.sbuf_tensor("ln2", [128, NT_D, 1024], F16))

    def dprld(sync):
        c.sIn0.inc(sync.dma_start(out=c.dpr[:], in_=c.dpr_scr[:, 0:2 * DW]), 16)
    blk.sync(dprld)

    _emit_ln2_hbm(c, blk, ln2.ap())
    _barrier(c, blk)
    if _DBG:
        def dmpl(sync):
            c.sIn0.inc(sync.dma_start(out=c.dbg_ln2[:, :],
                                      in_=ln2.ap()[:, :, :].rearrange("p k f -> p (k f)")), 16)
            sync.wait_ge(c.sIn0.h, c.sIn0.n)
        blk.sync(dmpl)
        _barrier(c, blk)

    with nc.sbuf_tensor("KTl", [128, 4 * 1024], F16) as KTl, \
         nc.sbuf_tensor("VTl", [128, 4 * 1024], F16) as VTl, \
         nc.sbuf_tensor("Vtok", [128, 8 * 512], BF16) as Vtok, \
         nc.sbuf_tensor("wraw", [128, DM], F32) as wraw, \
         nc.sbuf_tensor("wf16", [128, DM], F16) as wf16, \
         nc.sbuf_tensor("wTt", [128, 16 * 128], F16) as wT:

        def proj(wsrc, nm, out_cb):
            base_i = c.sIn0.n
            base_ac, base_pt, base_vt = c.sAc.n, c.sPt.n, c.sVt.n
            base_pe, base_ve = c.sPe.n, c.sVe.n

            def dmas(sync):
                for m in range(nm):
                    sync.wait_ge(c.sAc.h, base_ac + m)
                    c.sIn0.inc(sync.dma_start(out=wraw[:, :],
                                              in_=wsrc[128 * m:128 * (m + 1), :]), 16)
            blk.sync(dmas)

            def casts(scalar):
                for m in range(nm):
                    scalar.wait_ge(c.sIn0.h, base_i + 16 * (m + 1))
                    scalar.wait_ge(c.sPt.h, base_pt + 16 * m)
                    c.sAc.inc(scalar.copy(wf16[:, :], wraw[:, :]))
            blk.scalar(casts)

            def transp(tensor):
                for m in range(nm):
                    tensor.wait_ge(c.sAc.h, base_ac + m + 1)
                    for k in range(16):
                        idx = m * 16 + k
                        tensor.wait_ge(c.sVt.h, base_vt + max(idx - 1, 0))
                        c.sPt.inc(tensor.transpose(
                            c.pt[idx % 2][:, 0:128],
                            wf16[:, 128 * k: 128 * (k + 1)],
                            c.ident_sb[:]))
                    tensor.wait_ge(c.sVt.h, base_vt + 16 * (m + 1))
                    tensor.wait_ge(c.sVe.h, base_ve + max(m - 1, 0))
                    for half in range(2):
                        for k in range(16):
                            ins = tensor.matmul(
                                c.ps[2 * (m % 2) + half][:, :], wT[:, 128 * k:128 * (k + 1)],
                                ln2[:, k, 512 * half:512 * (half + 1)],
                                start=(k == 0), stop=(k == 15))
                    c.sPe.inc(ins)
            blk.tensor(transp)

            def vec(vector):
                def oevac(m):
                    vector.wait_ge(c.sPe.h, base_pe + m + 1)
                    ins = out_cb(vector, m)
                    c.sVe.inc(ins)
                for m in range(nm):
                    for k in range(16):
                        idx = m * 16 + k
                        vector.wait_ge(c.sPt.h, base_pt + idx + 1)
                        c.sVt.inc(vector.tensor_copy(wT[:, 128 * k:128 * (k + 1)],
                                                     c.pt[idx % 2][:, 0:128]))
                    if m >= 1:
                        oevac(m - 1)
                oevac(nm - 1)
            blk.vector(vec)

        def q_out(vector, m):
            for half in range(2):
                ins = vector.tensor_scalar(
                    c.QT[:, 1024 * m + 512 * half: 1024 * m + 512 * (half + 1)],
                    c.ps[2 * (m % 2) + half][:, :], SCALE_QK, None, Alu.mult)
            return ins

        def k_out(vector, m):
            for half in range(2):
                ins = vector.tensor_copy(
                    KTl[:, 1024 * m + 512 * half: 1024 * m + 512 * (half + 1)],
                    c.ps[2 * (m % 2) + half][:, :])
            return ins

        def v_out(vector, m):
            for half in range(2):
                ins = vector.tensor_copy(
                    VTl[:, 1024 * m + 512 * half: 1024 * m + 512 * (half + 1)],
                    c.ps[2 * (m % 2) + half][:, :])
            return ins

        proj(c.wqd, 16, q_out)
        proj(c.wkd, 4, k_out)
        proj(c.wvd, 4, v_out)

        base_pt, base_vt = c.sPt.n, c.sVt.n

        def vtr(tensor):
            i = 0
            for g in range(4):
                for t in range(8):
                    tensor.wait_ge(c.sVt.h, base_vt + max(i - 1, 0))
                    c.sPt.inc(tensor.transpose(
                        c.pt[i % 2][:, 0:128],
                        VTl[:, 1024 * g + 128 * t: 1024 * g + 128 * (t + 1)],
                        c.ident_sb[:]))
                    i += 1
        blk.tensor(vtr)

        def vtre(vector):
            i = 0
            for g in range(4):
                for t in range(8):
                    vector.wait_ge(c.sPt.h, base_pt + i + 1)
                    c.sVt.inc(vector.tensor_copy(
                        Vtok[:, 512 * t + 128 * g: 512 * t + 128 * (g + 1)],
                        c.pt[i % 2][:, 0:128]))
                    i += 1
        blk.vector(vtre)
        tvt = c.sVt.n

        def stage(scalar, tvt=tvt):
            scalar.wait_ge(c.sVt.h, tvt)
            c.sSc.inc(scalar.dma_start(out=c.cc_in[0],
                                       in_=KTl.ap()[:, :].bitcast(U16)), 16)
            c.sSc.inc(scalar.dma_start(out=c.cc_in[1],
                                       in_=Vtok.ap()[:, :].bitcast(U16)), 16)
        blk.scalar(stage)
        tst = c.sSc.n

        def coll(gpsimd, tst=tst):
            gpsimd.wait_ge(c.sSc.h, tst)
            c.sCc.inc(gpsimd.collective_compute(
                "AllGather", Alu.bypass,
                replica_groups=[[0, 1, 2, 3], [4, 5, 6, 7]],
                ins=[c.cc_in[:, :, :]], outs=[c.cc_out[:, :, :, :]]))
        blk.gpsimd(coll)

    c.reg_ln2.close()
    Ek = c.reg_kv.enter_context
    c.KT = Ek(nc.sbuf_tensor("KT", [128, 4 * 4096], F16))
    c.Vsb = Ek(nc.sbuf_tensor("Vsb", [128, 32 * 512], BF16))
    c.dpr2 = Ek(nc.sbuf_tensor("dpr2", [128, 1024], F16))
    c.cmax = Ek(nc.sbuf_tensor("cmax", [128, 2 * NH], F16))
    c.denr = Ek(nc.sbuf_tensor("denr", [1, 512], F32))
    c.oraw = Ek(nc.sbuf_tensor("oraw", [128, 2 * 512], F16))
    tcc = c.sCc.n

    def gload(scalar, tcc=tcc):
        scalar.wait_ge(c.sCc.h, tcc)
        c.sSc.inc(scalar.dma_start(out=c.dpr2[:],
                                   in_=c.dpr_scr[:, 2 * DW:2 * DW + 1024]), 16)
        for rk in range(4):
            for g in range(4):
                c.sSc.inc(scalar.dma_start(
                    out=c.KT.ap()[:, 4096 * g + 1024 * rk: 4096 * g + 1024 * (rk + 1)]
                    .bitcast(U16),
                    in_=c.cc_out[rk, 0][:, 1024 * g:1024 * (g + 1)]), 16)
            c.sSc.inc(scalar.dma_start(
                out=c.Vsb.ap()[:, 4096 * rk: 4096 * (rk + 1)].bitcast(U16),
                in_=c.cc_out[rk, 1][:, :]), 16)
    blk.scalar(gload)
    tg = c.sSc.n

    base_pn, base_vn = c.sPn.n, c.sVn.n

    def ksamp(vector, tg=tg):
        vector.wait_ge(c.sSc.h, tg)
        ins = None
        for g in range(4):
            src = c.KT.ap()[:, 4096 * g:4096 * (g + 1)].rearrange(
                "p (j s) -> p s j", s=32)[:, 0, :]
            ins = vector.tensor_copy(c.qsq[:, 128 * g:128 * (g + 1)], src)
        c.sVn.inc(ins)
    blk.vector(ksamp)

    tqv = c.sVe.n

    def prep_mm(tensor):
        tensor.wait_ge(c.sVe.h, tqv)
        for hc in range(2 * NH):
            h, ch = hc // 2, hc % 2
            g = h // 4
            tensor.wait_ge(c.sVn.h, base_vn + 1)
            if hc >= 1:
                tensor.wait_ge(c.sVn.h, base_vn + 1 + hc)
            for qs in range(4):
                ins = tensor.matmul(
                    c.ps[2][:, 128 * qs:128 * (qs + 1)],
                    c.QT[:, 1024 * h + 512 * ch + 128 * qs: 1024 * h + 512 * ch + 128 * (qs + 1)],
                    c.qsq[:, 128 * g:128 * (g + 1)], start=True, stop=True)
            c.sPn.inc(ins)
    blk.tensor(prep_mm)

    def prep_v(vector):
        for hc in range(2 * NH):
            h, ch = hc // 2, hc % 2
            vector.wait_ge(c.sPn.h, base_pn + hc + 1)
            for qs in range(4):
                vector.scalar_tensor_tensor(
                    c.smb[:, 512 + 128 * qs:512 + 128 * (qs + 1)],
                    c.dpr2[:, 128 * (4 * ch + qs):128 * (4 * ch + qs + 1)],
                    -SLOPES[h], c.ps[2][:, 128 * qs:128 * (qs + 1)],
                    Alu.mult, Alu.add)
            ins = vector.reduce_max(c.cmax[:, hc:hc + 1], c.smb[:, 512:1024], axis=AX)
            c.sVn.inc(ins)
    blk.vector(prep_v)
    tnv = c.sVn.n

    def redt(tensor, tnv=tnv):
        tensor.wait_ge(c.sVn.h, tnv)
        c.sPn.inc(tensor.transpose(c.pt[0][0:32, 0:128],
                                   c.cmax[:, :], c.ident_sb[:]))
    blk.tensor(redt)
    tnp = c.sPn.n

    def redv(vector, tnp=tnp):
        vector.wait_ge(c.sPn.h, tnp)
        c.sVn.inc(vector.reduce_max(c.cmax[0:32, 0:1],
                                    c.pt[0][0:32, 0:128], axis=AX))
    blk.vector(redv)
    tnv2 = c.sVn.n

    def redt2(tensor, tnv2=tnv2):
        tensor.wait_ge(c.sVn.h, tnv2)
        c.sPn.inc(tensor.transpose(c.pt[1][0:1, 0:128], c.cmax[:, 0:1], c.ident_sb[:]))
    blk.tensor(redt2)
    tnp2 = c.sPn.n

    def redv2(vector, tnp2=tnp2):
        vector.wait_ge(c.sPn.h, tnp2)
        c.sVn.inc(vector.tensor_scalar(c.rowC[0:1, 0:32], c.pt[1][0:1, 0:32],
                                       -1.0, None, Alu.mult))
    blk.vector(redv2)
    tnv3 = c.sVn.n

    def bc(tensor, tnv3=tnv3):
        tensor.wait_ge(c.sVn.h, tnv3)
        c.sPe.inc(tensor.matmul(c.ps[2][:, 0:32], c.onerow_32_sb[:],
                                c.rowC[0:1, 0:32], start=True, stop=True))
    blk.tensor(bc)
    tp = c.sPe.n

    def ev(scalar, tp=tp):
        scalar.wait_ge(c.sPe.h, tp)
        c.sAc.inc(scalar.copy(c.negM[:, :], c.ps[2][:, 0:32]))
    blk.scalar(ev)
    _barrier(c, blk)


def _emit_attn_main(c, blk):
    M = 2 * NH
    NJ = NKT
    base_ps, base_po, base_pr = c.sPs.n, c.sPo.n, c.sPr.n
    base_vs, base_vo, base_vm = c.sVs.n, c.sVo.n, c.sVm.n
    base_ae, base_ar = c.sAe.n, c.sAr.n
    base_q, base_dc = c.sPw0.n, c.sPw1.n
    RBJ = 6

    def rbmm(tensor, m):
        tensor.wait_ge(c.sPw0.h, base_q + m + 1)
        tensor.wait_ge(c.sAr.h, base_ar + m)
        c.sPr.inc(tensor.matmul(c.ps[5][:, :], c.onerow_32_sb[:],
                                c.rowB[0:1, 0:512], start=True, stop=True))

    def pe(tensor):
        for m in range(M):
            h, ch = m // 2, m % 2
            g = h // 4
            qsl = slice(1024 * h + 512 * ch, 1024 * h + 512 * (ch + 1))
            po = c.ps[2 + m % 2]
            for j in range(NJ + 2):
                gidx = m * NJ + j
                if j < NJ:
                    if gidx >= 2:
                        tensor.wait_ge(c.sVs.h, base_vs + gidx - 1)
                    c.sPs.inc(tensor.matmul(
                        c.ps[j % 2][:, :],
                        c.KT[:, 4096 * g + 128 * j: 4096 * g + 128 * (j + 1)],
                        c.QT[:, qsl], start=True, stop=True))
                if j == RBJ and m >= 1:
                    rbmm(tensor, m - 1)
                if j >= 2:
                    jj = j - 2
                    gj = m * NJ + jj
                    tensor.wait_ge(c.sAe.h, base_ae + gj + 1)
                    if jj == 0 and m >= 2:
                        tensor.wait_ge(c.sVm.h, base_vm + m - 1)
                    esl = c.Eb[:, 512 * (gj % 4):512 * (gj % 4 + 1)]
                    tensor.matmul(
                        po[:, :],
                        c.Vsb[:, 512 * jj + 128 * g: 512 * jj + 128 * (g + 1)],
                        esl, start=(jj == 0), stop=(jj == NJ - 1))
                    if jj == 0 and m >= 1:
                        tensor.wait_ge(c.sPw1.h, base_dc + m)
                    c.sPo.inc(tensor.matmul(
                        c.ps[4][0:1, :], c.onecol_bf_sb[:], esl,
                        start=(jj == 0), stop=(jj == NJ - 1)))

        rbmm(tensor, M - 1)
    blk.tensor(pe)

    def boundary(scalar, mp):
        scalar.wait_ge(c.sPo.h, base_po + 32 * (mp + 1))
        c.sPw1.inc(scalar.copy(c.denr[0:1, :], c.ps[4][0:1, :]))
        yield
        c.sVm.inc(scalar.copy(c.oraw[:, 512 * (mp % 2):512 * (mp % 2 + 1)],
                              c.ps[2 + mp % 2][:, :]))
        yield
        scalar.activation(c.rowA[0:1, 0:512], c.denr[0:1, :], Act.Ln)
        yield
        c.sPw0.inc(scalar.activation(c.rowB[0:1, 0:512], c.rowA[0:1, 0:512],
                                     Act.Exp, scale=-1.0))
        yield
        scalar.wait_ge(c.sPr.h, base_pr + mp + 1)
        if mp >= 1:
            scalar.wait_ge(c.sVo.h, base_vo + mp)
        c.sAr.inc(scalar.copy(c.aB[:, 0:512], c.ps[5][:, :]))

    def sc(scalar):
        bnd = None
        for m in range(M):
            h = m // 2
            if m >= 1:
                bnd = boundary(scalar, m - 1)
                next(bnd)
            for j in range(NJ):
                gidx = m * NJ + j
                scalar.wait_ge(c.sVs.h, base_vs + gidx + 1)
                if gidx >= 4:
                    scalar.wait_ge(c.sPo.h, base_po + gidx - 3)
                c.sAe.inc(scalar.activation(
                    c.Eb[:, 512 * (gidx % 4):512 * (gidx % 4 + 1)],
                    c.smb[:, 512 * (gidx % 4):512 * (gidx % 4 + 1)],
                    Act.Exp, bias=c.negM[:, m:m + 1]))
                if bnd is not None and j in (0, 1, 2, 7):
                    try:
                        next(bnd)
                    except StopIteration:
                        bnd = None
        bnd = boundary(scalar, M - 1)
        for _ in range(5):
            try:
                next(bnd)
            except StopIteration:
                break
    blk.scalar(sc)

    def ve(vector):
        omult = [None]

        def do_omult(mp):
            vector.wait_ge(c.sAr.h, base_ar + mp + 1)
            hp, chp = mp // 2, mp % 2
            c.sVo.inc(vector.tensor_tensor(
                c.oT[:, 1024 * hp + 512 * chp: 1024 * hp + 512 * (chp + 1)],
                c.oraw[:, 512 * (mp % 2):512 * (mp % 2 + 1)],
                c.aB[:, 0:512], Alu.mult))
        for m in range(M):
            h, ch = m // 2, m % 2
            for j in range(NJ):
                gidx = m * NJ + j
                vector.wait_ge(c.sPs.h, base_ps + gidx + 1)
                if gidx >= 4:
                    vector.wait_ge(c.sAe.h, base_ae + gidx - 3)
                dsl = c.dpr[:, DW * ch + 128 * (31 - j): DW * ch + 128 * (31 - j) + 512]
                c.sVs.inc(vector.scalar_tensor_tensor(
                    c.smb[:, 512 * (gidx % 4):512 * (gidx % 4 + 1)], dsl, -SLOPES[h],
                    c.ps[j % 2][:, :], Alu.mult, Alu.add))
                if j == 20 and m >= 1:
                    do_omult(m - 1)
        do_omult(M - 1)
    blk.vector(ve)
    _barrier(c, blk)


def _emit_wo(c, blk, nc):
    with nc.sbuf_tensor("wraw2", [128, 2 * DM], F32) as wraw, \
         nc.sbuf_tensor("wf162", [128, 2 * DM], F16) as wf16, \
         nc.sbuf_tensor("wT2", [128, 16 * 128], F16) as wT:
        base_i = [c.sIn0.n, c.sIn1.n]
        base_ac, base_pt, base_vt = c.sAc.n, c.sPt.n, c.sVt.n
        base_pe, base_ve = c.sPe.n, c.sVe.n

        def dmas(sync):
            for m in range(NT_D):
                par = m % 2
                sem = [c.sIn0, c.sIn1][par]
                if m >= 2:
                    sync.wait_ge(c.sAc.h, base_ac + m - 1)
                sem.inc(sync.dma_start(out=wraw[:, DM * par:DM * (par + 1)],
                                       in_=c.wod[128 * m:128 * (m + 1), :]), 16)
        blk.sync(dmas)

        def casts(scalar):
            for m in range(NT_D):
                par = m % 2
                sem = [c.sIn0, c.sIn1][par]
                scalar.wait_ge(sem.h, base_i[par] + 16 * (m // 2 + 1))
                if m >= 2:
                    scalar.wait_ge(c.sPt.h, base_pt + 16 * (m - 1))
                c.sAc.inc(scalar.copy(wf16[:, DM * par:DM * (par + 1)],
                                      wraw[:, DM * par:DM * (par + 1)]))
        blk.scalar(casts)

        def transp(tensor):
            for m in range(NT_D):
                par = m % 2
                tensor.wait_ge(c.sAc.h, base_ac + m + 1)
                for k in range(16):
                    idx = m * 16 + k
                    if idx >= 2:
                        tensor.wait_ge(c.sVt.h, base_vt + idx - 1)
                    c.sPt.inc(tensor.transpose(
                        c.pt[idx % 2][:, 0:128],
                        wf16[:, DM * par + 128 * k: DM * par + 128 * (k + 1)],
                        c.ident_sb[:]))
                tensor.wait_ge(c.sVt.h, base_vt + 16 * (m + 1))
                if m >= 2:
                    tensor.wait_ge(c.sVe.h, base_ve + m - 1)
                for half in range(2):
                    for k in range(16):
                        ins = tensor.matmul(
                            c.ps[2 * (m % 2) + half][:, :], wT[:, 128 * k:128 * (k + 1)],
                            c.oT[:, 1024 * k + 512 * half: 1024 * k + 512 * (half + 1)],
                            start=(k == 0), stop=(k == 15))
                c.sPe.inc(ins)
        blk.tensor(transp)

        def vec(vector):
            def resid(m):
                vector.wait_ge(c.sPe.h, base_pe + m + 1)
                for half in range(2):
                    xsl = c.xT[:, T * m + 512 * half: T * m + 512 * (half + 1)]
                    ins = vector.tensor_tensor(xsl, xsl, c.ps[2 * (m % 2) + half][:, :], Alu.add)
                c.sVe.inc(ins)
            for m in range(NT_D):
                for k in range(16):
                    idx = m * 16 + k
                    vector.wait_ge(c.sPt.h, base_pt + idx + 1)
                    c.sVt.inc(vector.tensor_copy(wT[:, 128 * k:128 * (k + 1)],
                                                 c.pt[idx % 2][:, 0:128]))
                if m >= 1:
                    resid(m - 1)
            resid(NT_D - 1)
        blk.vector(vec)
    _barrier(c, blk)


def _stage_core(core, x, wq, wk, wv, wo, gupk_i32, gusc, dnpk_i32, dnsc):
    b = core // 4
    t0 = T * (core % 4)
    xs = np.ascontiguousarray(x[b, t0:t0 + T, :][:, PERM_D].T)

    rows = np.empty(2 * I_FFN, dtype=np.int64)
    for kap in range(NT_I):
        sl = PERM_I[128 * kap:128 * (kap + 1)]
        rows[256 * kap:256 * kap + 128] = sl
        rows[256 * kap + 128:256 * kap + 256] = I_FFN + sl

    delt = np.empty((2, DW, 128), dtype=np.float32)
    bigm = np.empty((2, DW, 128), dtype=np.float16)
    pp = np.arange(128)[:, None]
    ff = np.arange(128)[None, :]
    for ch in range(2):
        q0 = t0 + 512 * ch
        for i in range(NDELT):
            dlt = q0 - 128 * (31 - i)
            raw = (dlt + ff - pp).astype(np.float64)
            delt[ch, 128 * i:128 * (i + 1), :] = np.where(raw > 0, raw, 1e-20)
            bigm[ch, 128 * i:128 * (i + 1), :] = np.where(raw < 0, BIG, 0.0)

    dlt2 = np.empty((8 * 128, 128), dtype=np.float32)
    msk2 = np.empty((8 * 128, 128), dtype=np.float16)
    jj = 32 * np.arange(128)[None, :]
    for i in range(8):
        ch, qs = i // 4, i % 4
        qg = t0 + 512 * ch + 128 * qs + np.arange(128)[:, None]
        raw = (qg - jj).astype(np.float64)
        dlt2[128 * i:128 * (i + 1), :] = np.where(raw > 0, raw, 1e-20)
        msk2[128 * i:128 * (i + 1), :] = np.where(raw < 0, BIG, 0.0)

    return {
        "xs": xs, "dlt2": dlt2, "msk2": msk2,
        "wqd": np.ascontiguousarray(wq[:, PERM_D]),
        "wkd": np.ascontiguousarray(wk[:, PERM_D]),
        "wvd": np.ascontiguousarray(wv[:, PERM_D]),
        "wod": np.ascontiguousarray(wo[PERM_D, :]),
        "gupk": np.ascontiguousarray(gupk_i32.view(np.uint16)[rows, :]),
        "gusc": np.ascontiguousarray(gusc[:, rows]),
        "dnpk": np.ascontiguousarray(dnpk_i32.view(np.uint16)[PERM_D, :]),
        "dnsc": np.ascontiguousarray(dnsc[:, PERM_D]),
        "delt": delt, "bigm": bigm,
        "ident": np.eye(128, dtype=np.float16),
        "onecol_bf": np.ones((128, 1), dtype=ml_dtypes.bfloat16),
        "onecol_f16": np.ones((128, 1), dtype=np.float16),
        "onerow_32": np.ones((1, 128), dtype=np.float32),
    }


def kernel(L0_attn_x, L0_attn_Wq, L0_attn_Wk, L0_attn_Wv, L0_attn_Wo,
           L0_ffn_gate_up_packed, L0_ffn_gate_up_scales,
           L0_ffn_down_packed, L0_ffn_down_scales):
    x = np.asarray(L0_attn_x, dtype=np.float32)
    args = (np.asarray(L0_attn_Wq), np.asarray(L0_attn_Wk),
            np.asarray(L0_attn_Wv), np.asarray(L0_attn_Wo),
            np.asarray(L0_ffn_gate_up_packed), np.asarray(L0_ffn_gate_up_scales),
            np.asarray(L0_ffn_down_packed), np.asarray(L0_ffn_down_scales))
    in_maps = [_stage_core(cc, x, *args) for cc in range(NCORE)]
    nc = build_program()
    res = run_bass_kernel_spmd(nc, in_maps, core_ids=list(range(NCORE)))
    out = np.empty((B, S, DM), dtype=np.float32)
    for cc in range(NCORE):
        b = cc // 4
        t0 = T * (cc % 4)
        out[b, t0:t0 + T, PERM_D] = res.results[cc]["xo"]
    return out


# version 20 (final)
# speedup vs baseline: 1.0289x; 1.0289x over previous
import math
from contextlib import ExitStack

import numpy as np
import ml_dtypes

import concourse.bass as bass
import concourse.mybir as mybir
from concourse.bass_utils import run_bass_kernel_spmd

F32 = mybir.dt.float32
F16 = mybir.dt.float16
BF16 = mybir.dt.bfloat16
U16 = mybir.dt.uint16
Alu = mybir.AluOpType
Act = mybir.ActivationFunctionType
AX = mybir.AxisListType.X

B, S, DM = 2, 4096, 2048
NCORE = 8
T = (B * S) // NCORE
I_FFN = 5632
NT_D = DM // 128
NT_I = I_FFN // 128
NH, DH = 16, 128
SCALE_QK = 1.0 / math.sqrt(DH)
SLOPES = [2.0 ** (-0.5 * (h + 1)) for h in range(NH)]
H_SCALE = 1.0 / 16.0
EPS_RMS = 1e-6 * H_SCALE * H_SCALE
EPS_LN = 1e-5
BIG = 60000.0
NKT = S // 128
NDELT = NKT + 3
GU_W = 22
LNHALF = math.log(0.5)
DW = NDELT * 128
_DBG = False
_STAGES = 7
_FFN_MODE = 3
_LN_MODE = 5
_FFN_SLICES = 2


def _perm(n):
    p = np.empty(n, dtype=np.int64)
    i = 0
    for c in range(n // 512):
        for r in range(4):
            for q in range(128):
                p[i] = 512 * c + 4 * q + r
                i += 1
    return p


PERM_D = _perm(DM)
PERM_I = _perm(I_FFN)


class Sem:
    def __init__(self, es, nc, name):
        self.h = es.enter_context(nc.semaphore(name))
        self.n = 0

    def inc(self, ins, k=1):
        ins.then_inc(self.h, k)
        self.n += k
        return self.n


class Ctx:
    pass


def build_program():
    nc = bass.Bass("TRN2", target_bir_lowering=False, detect_race_conditions=False)
    c = Ctx()
    c.nc = nc
    ap = lambda t: t.ap()
    c.xs = ap(nc.dram_tensor("xs", [DM, T], F32, kind="ExternalInput"))
    c.wqd = ap(nc.dram_tensor("wqd", [DM, DM], F32, kind="ExternalInput"))
    c.wkd = ap(nc.dram_tensor("wkd", [512, DM], F32, kind="ExternalInput"))
    c.wvd = ap(nc.dram_tensor("wvd", [512, DM], F32, kind="ExternalInput"))
    c.wod = ap(nc.dram_tensor("wod", [DM, DM], F32, kind="ExternalInput"))
    c.gupk = ap(nc.dram_tensor("gupk", [2 * I_FFN, DM // 4], U16, kind="ExternalInput"))
    c.gusc = ap(nc.dram_tensor("gusc", [DM // 128, 2 * I_FFN], F16, kind="ExternalInput"))
    c.dnpk = ap(nc.dram_tensor("dnpk", [DM, I_FFN // 4], U16, kind="ExternalInput"))
    c.dnsc = ap(nc.dram_tensor("dnsc", [I_FFN // 128, DM], F16, kind="ExternalInput"))
    c.delt = ap(nc.dram_tensor("delt", [2, DW, 128], F32, kind="ExternalInput"))
    c.bigm = ap(nc.dram_tensor("bigm", [2, DW, 128], F16, kind="ExternalInput"))
    c.dlt2 = ap(nc.dram_tensor("dlt2", [8 * 128, 128], F32, kind="ExternalInput"))
    c.msk2 = ap(nc.dram_tensor("msk2", [8 * 128, 128], F16, kind="ExternalInput"))
    c.ident = ap(nc.dram_tensor("ident", [128, 128], F16, kind="ExternalInput"))
    c.onecol_bf = ap(nc.dram_tensor("onecol_bf", [128, 1], BF16, kind="ExternalInput"))
    c.onecol_f16 = ap(nc.dram_tensor("onecol_f16", [128, 1], F16, kind="ExternalInput"))
    c.onerow_32 = ap(nc.dram_tensor("onerow_32", [1, 128], F32, kind="ExternalInput"))
    c.xo = ap(nc.dram_tensor("xo", [DM, T], F32, kind="ExternalOutput"))
    c.gu_scr = ap(nc.dram_tensor("gu_scr", [NT_D, 128, 2 * I_FFN], F16, kind="Internal"))
    c.dn_scr = ap(nc.dram_tensor("dn_scr", [NT_I, 128, DM], F16, kind="Internal"))
    c.xspill = ap(nc.dram_tensor("xspill", [128, NT_D * T], F32, kind="Internal"))
    c.cc_in = ap(nc.dram_tensor("cc_in", [2, 128, 4096], U16, kind="Internal"))
    c.cc_out = ap(nc.dram_tensor("cc_out", [4, 2, 128, 4096], U16, kind="Internal"))
    c.dpr_scr = ap(nc.dram_tensor("dpr_scr", [128, 2 * DW + 1024], F16, kind="Internal"))
    if _DBG:
        c.dbg_o = ap(nc.dram_tensor("dbg_o", [128, NH * 1024], F16, kind="ExternalOutput"))
        c.dbg_x2 = ap(nc.dram_tensor("dbg_x2", [128, NT_D * T], F32, kind="ExternalOutput"))
        c.dbg_ln2 = ap(nc.dram_tensor("dbg_ln2", [128, NT_D * 1024], F16, kind="ExternalOutput"))
        c.dbg_qt = ap(nc.dram_tensor("dbg_qt", [128, NH * 1024], F16, kind="ExternalOutput"))
        c.dbg_x1 = ap(nc.dram_tensor("dbg_x1", [128, NT_D * T], F32, kind="ExternalOutput"))
        c.dbg_rec = ap(nc.dram_tensor("dbg_rec", [2 * NH, 512], F32, kind="ExternalOutput"))
        c.dbg_nm = ap(nc.dram_tensor("dbg_nm", [128, 2 * NH], F32, kind="ExternalOutput"))

    with ExitStack() as es:
        E = es.enter_context
        for name in ["sIn0", "sIn1", "sSc", "sCc", "sPe", "sPs", "sPo", "sPr",
                     "sPw0", "sPw1", "sVe", "sVs", "sVr", "sVo", "sAc", "sAe",
                     "sAr", "sBar", "sWr0", "sWr1", "sVd", "sPp", "sAs", "sVh",
                     "sPy", "sAy", "sVq", "sPt", "sVt", "sPn", "sVn", "sVm"]:
            setattr(c, name, Sem(es, nc, name))

        c.ident_sb = E(nc.sbuf_tensor("ident_sb", [128, 128], F16))
        c.onecol_bf_sb = E(nc.sbuf_tensor("onecol_bf_sb", [128, 1], BF16))
        c.onecol_f16_sb = E(nc.sbuf_tensor("onecol_f16_sb", [128, 1], F16))
        c.onerow_32_sb = E(nc.sbuf_tensor("onerow_32_sb", [1, 128], F32))
        c.rowA = E(nc.sbuf_tensor("rowA", [1, 1024], F32))
        c.rowB = E(nc.sbuf_tensor("rowB", [1, 1024], F32))
        c.rowC = E(nc.sbuf_tensor("rowC", [1, 256], F32))
        c.rowD = E(nc.sbuf_tensor("rowD", [1, 256], F32))
        c.aB = E(nc.sbuf_tensor("aB", [128, 1024], F32))
        c.bB = E(nc.sbuf_tensor("bB", [128, 1024], F32))
        c.hsil = E(nc.sbuf_tensor("hsil", [128, 2 * 512], F16))
        c.tmpv = E(nc.sbuf_tensor("tmpv", [128, 512], F32))
        c.ysq = E(nc.sbuf_tensor("ysq", [128, 2 * 512], BF16))
        c.xb = E(nc.sbuf_tensor("xb", [128, 2 * 512], BF16))
        c.xsq = E(nc.sbuf_tensor("xsq", [128, 2 * 512], BF16))
        c.fence = E(nc.sbuf_tensor("fence", [1, 8], F32))
        c.ps = [E(nc.psum_tensor(f"ps{i}", [128, 512], F32)) for i in range(6)]
        c.pt = [E(nc.psum_tensor(f"pt{i}", [128, 1024], F16)) for i in range(2)]

        c.block = E(nc.Block())
        blk = c.block

        _emit_consts(c, blk)
        _emit_dprep(c, blk, nc)
        if _STAGES >= 2:
            _emit_dequant(c, blk, nc)

        if _STAGES >= 3:
            with nc.sbuf_tensor("xT1", [128, NT_D * T], F32, side="right") as xT:
                c.xT = xT
                _emit_load_x(c, blk)
                _emit_ffn(c, blk, nc, tag="A")

                def spill(scalar):
                    scalar.wait_ge(c.sVe.h, c.sVe.n)
                    c.sSc.inc(scalar.dma_start(out=c.xspill[:, :], in_=c.xT[:]), 16)
                    if _DBG:
                        c.sSc.inc(scalar.dma_start(out=c.dbg_x1[:, :], in_=c.xT[:]), 16)
                        scalar.wait_ge(c.sSc.h, c.sSc.n)
                blk.scalar(spill)
        if _STAGES < 4:
            _finish_partial(c, blk)
            return nc
        c.reg_o = ExitStack()
        c.reg_attn = ExitStack()
        c.reg_q = ExitStack()
        c.reg_ln2 = ExitStack()
        c.reg_kv = ExitStack()
        Ea = c.reg_attn.enter_context
        c.oT = c.reg_o.enter_context(nc.sbuf_tensor("oT", [128, NH * 1024], F16))
        c.dpr = Ea(nc.sbuf_tensor("dpr", [128, 2 * DW], F16))
        c.negM = Ea(nc.sbuf_tensor("negM", [128, 2 * NH], F32))
        c.smb = Ea(nc.sbuf_tensor("smb", [128, 4 * 512], F32))
        c.Eb = Ea(nc.sbuf_tensor("Eb", [128, 4 * 512], BF16))
        c.qsq = Ea(nc.sbuf_tensor("qsq", [128, 512], F16))
        c.xstg = Ea(nc.sbuf_tensor("xstg", [128, 2 * 1024], F32))
        c.QT = c.reg_q.enter_context(nc.sbuf_tensor("QT", [128, NH * 1024], F16))
        _emit_attn_pre(c, blk, nc)
        if _STAGES < 5:
            c.reg_kv.close(); c.reg_q.close(); c.reg_attn.close(); c.reg_o.close()
            _finish_partial(c, blk)
            return nc
        _emit_attn_main(c, blk)
        if _DBG:
            def dmp(sync):
                c.sIn0.inc(sync.dma_start(out=c.dbg_o[:, :], in_=c.oT[:]), 16)
                c.sIn0.inc(sync.dma_start(out=c.dbg_qt[:, :], in_=c.QT[:]), 16)
                c.sIn0.inc(sync.dma_start(out=c.dbg_nm[:, :], in_=c.negM[:]), 16)
                sync.wait_ge(c.sIn0.h, c.sIn0.n)
            blk.sync(dmp)
            _barrier(c, blk)
        c.reg_kv.close()
        c.reg_q.close()
        c.reg_attn.close()
        if _STAGES < 6:
            c.reg_o.close()
            _finish_partial(c, blk)
            return nc
        with nc.sbuf_tensor("xT2", [128, NT_D * T], F32, side="right") as xT2:
            c.xT = xT2
            _emit_reload_x(c, blk)
            _emit_wo(c, blk, nc)
            if _DBG:
                def dmp2(sync):
                    c.sIn0.inc(sync.dma_start(out=c.dbg_x2[:, :], in_=c.xT[:]), 16)
                    sync.wait_ge(c.sIn0.h, c.sIn0.n)
                blk.sync(dmp2)
                _barrier(c, blk)
            c.reg_o.close()
            if _STAGES >= 7:
                _emit_ffn(c, blk, nc, tag="B")
            _emit_store(c, blk)
    return nc


def _finish_partial(c, blk):
    _barrier(c, blk)

    def body(scalar):
        scalar.wait_ge(c.sBar.h, c.sBar.n)
    blk.scalar(body)


def _barrier(c, blk):
    c._bar = getattr(c, "_bar", 0) + 1
    tgt = 5 * c._bar

    def b_sync(eng):
        eng.nop().then_inc(c.sBar.h, 1)
        eng.wait_ge(c.sBar.h, tgt)

    def b_vec(eng):
        eng.tensor_copy(c.fence[0:1, 1:2], c.fence[0:1, 0:1]).then_inc(c.sBar.h, 1)
        eng.wait_ge(c.sBar.h, tgt)

    def b_sca(eng):
        eng.copy(c.fence[0:1, 3:4], c.fence[0:1, 2:3]).then_inc(c.sBar.h, 1)
        eng.wait_ge(c.sBar.h, tgt)

    def b_pe(eng):
        eng.ldweights(c.ident_sb[:, 0:1]).then_inc(c.sBar.h, 1)
        eng.wait_ge(c.sBar.h, tgt)

    def b_gp(eng):
        eng.tensor_copy(c.fence[0:1, 5:6], c.fence[0:1, 4:5]).then_inc(c.sBar.h, 1)
        eng.wait_ge(c.sBar.h, tgt)

    blk.sync(b_sync)
    blk.vector(b_vec)
    blk.scalar(b_sca)
    blk.tensor(b_pe)
    blk.gpsimd(b_gp)
    c.sBar.n += 5


def _emit_consts(c, blk):
    def body(sync):
        for dst, src in [(c.ident_sb, c.ident), (c.onecol_bf_sb, c.onecol_bf),
                         (c.onecol_f16_sb, c.onecol_f16), (c.onerow_32_sb, c.onerow_32)]:
            c.sIn0.inc(sync.dma_start(out=dst[:], in_=src[:, :]), 16)
    blk.sync(body)
    tin = c.sIn0.n

    def vinit(vector):
        vector.memset(c.fence[0:1, 0:2], 0.0)
    blk.vector(vinit)

    def sinit(scalar):
        scalar.memzero(c.fence[0:1, 2:4])
    blk.scalar(sinit)

    def ginit(gpsimd):
        gpsimd.memset(c.fence[0:1, 4:6], 0.0)
    blk.gpsimd(ginit)

    def peinit(tensor, tin=tin):
        tensor.wait_ge(c.sIn0.h, tin)
    blk.tensor(peinit)


def _emit_dprep(c, blk, nc):
    with nc.sbuf_tensor("dta", [128, DW], F32) as dta, \
         nc.sbuf_tensor("dtb", [128, DW], F32) as dtb, \
         nc.sbuf_tensor("dtd", [128, DW], F16) as dtd, \
         nc.sbuf_tensor("dtc", [128, DW], F16) as dtc:
        for ch in range(2):
            def ld(sync, ch=ch):
                sync.wait_ge(c.sVe.h, c.sVe.n)
                c.sIn0.inc(sync.dma_start(
                    out=dta.ap()[:, :].rearrange("p (i f) -> p i f", f=128),
                    in_=c.delt[ch].rearrange("(i p) f -> p i f", p=128)), 16)
                c.sIn0.inc(sync.dma_start(
                    out=dtc.ap()[:, :].rearrange("p (i f) -> p i f", f=128),
                    in_=c.bigm[ch].rearrange("(i p) f -> p i f", p=128)), 16)
            blk.sync(ld)
            tin = c.sIn0.n

            def sc(scalar, tin=tin):
                scalar.wait_ge(c.sIn0.h, tin)
                scalar.activation(dtb[:], dta[:], Act.Ln)
                c.sAc.inc(scalar.activation(dta[:], dtb[:], Act.Exp, scale=0.5))
            blk.scalar(sc)
            ta = c.sAc.n

            def ve(vector, ch=ch, ta=ta):
                vector.wait_ge(c.sAc.h, ta)
                vector.wait_ge(c.sWr0.h, c.sWr0.n)
                c.sVe.inc(vector.tensor_tensor(dtd[:], dta[:], dtc[:], Alu.add))
            blk.vector(ve)
            tvv = c.sVe.n

            def wr(sync, ch=ch, tvv=tvv):
                sync.wait_ge(c.sVe.h, tvv)
                c.sWr0.inc(sync.dma_start(
                    out=c.dpr_scr[:, ch * DW:(ch + 1) * DW], in_=dtd[:]), 16)
            blk.sync(wr)

        def ld2(sync):
            sync.wait_ge(c.sVe.h, c.sVe.n)
            c.sIn0.inc(sync.dma_start(
                out=dta[:, 0:1024].rearrange("p (i f) -> p i f", f=128),
                in_=c.dlt2.rearrange("(i p) f -> p i f", p=128)), 16)
            c.sIn0.inc(sync.dma_start(
                out=dtc[:, 0:1024].rearrange("p (i f) -> p i f", f=128),
                in_=c.msk2.rearrange("(i p) f -> p i f", p=128)), 16)
        blk.sync(ld2)
        tin2 = c.sIn0.n

        def sc2(scalar, tin2=tin2):
            scalar.wait_ge(c.sIn0.h, tin2)
            scalar.activation(dtb[:, 0:1024], dta[:, 0:1024], Act.Ln)
            c.sAc.inc(scalar.activation(dta[:, 0:1024], dtb[:, 0:1024], Act.Exp, scale=0.5))
        blk.scalar(sc2)
        ta2x = c.sAc.n

        def ve2(vector, ta2x=ta2x):
            vector.wait_ge(c.sAc.h, ta2x)
            c.sVe.inc(vector.tensor_tensor(dtd[:, 0:1024], dta[:, 0:1024],
                                           dtc[:, 0:1024], Alu.add))
        blk.vector(ve2)
        tvv2 = c.sVe.n

        def wr2(sync, tvv2=tvv2):
            sync.wait_ge(c.sVe.h, tvv2)
            c.sWr0.inc(sync.dma_start(
                out=c.dpr_scr[:, 2 * DW:2 * DW + 1024], in_=dtd[:, 0:1024]), 16)
        blk.sync(wr2)

        def drain(sync):
            sync.wait_ge(c.sWr0.h, c.sWr0.n)
        blk.sync(drain)
    _barrier(c, blk)


def _emit_dequant(c, blk, nc):
    W2 = 2 * I_FFN
    HW = I_FFN
    slices = [("gu", sl) for sl in range(4)] + [("dn", sl) for sl in range(11)]
    with nc.sbuf_tensor("dq_u", [128, 2 * W2], U16) as u, \
         nc.sbuf_tensor("dq_s", [128, 2 * W2], F16) as scl, \
         nc.sbuf_tensor("dq_t1", [128, HW], U16) as t1, \
         nc.sbuf_tensor("dq_t2", [128, HW], U16) as t2, \
         nc.sbuf_tensor("dq_pl", [128, 2 * W2], F16) as pl:
        base_sc = c.sSc.n
        base_vd = c.sVd.n
        base_wr = c.sWr0.n

        def ld(scalar):
            for t, (kind, sl) in enumerate(slices):
                W = W2 if kind == "gu" else DM
                pk = c.gupk if kind == "gu" else c.dnpk
                sc = c.gusc if kind == "gu" else c.dnsc
                par = t % 2
                if t >= 2:
                    scalar.wait_ge(c.sVd.h, base_vd + 4 * (t - 1))
                c.sSc.inc(scalar.dma_start_transpose(
                    u[:, par * W2: par * W2 + W], pk[:, 128 * sl:128 * (sl + 1)]), 16)
                for g in range(4):
                    c.sSc.inc(scalar.dma_start(
                        out=scl[32 * g:32 * (g + 1), par * W2: par * W2 + W],
                        in_=sc[4 * sl + g:4 * sl + g + 1, :].broadcast_to([32, W])), 16)
        blk.scalar(ld)

        def dq(vector):
            for t, (kind, sl) in enumerate(slices):
                W = W2 if kind == "gu" else DM
                par = t % 2
                vector.wait_ge(c.sSc.h, base_sc + 80 * (t + 1))
                for r in range(4):
                    pidx = 4 * t + r
                    ppar = pidx % 2
                    if pidx >= 2:
                        vector.wait_ge(c.sWr0.h, base_wr + 16 * (pidx - 1))
                    shs, shm = 12 - 4 * r, 9 - 4 * r
                    ops = Alu.logical_shift_left if shs >= 0 else Alu.logical_shift_right
                    opm = Alu.logical_shift_left if shm >= 0 else Alu.logical_shift_right
                    nh2 = (W + HW - 1) // HW
                    for hh in range(nh2):
                        w0 = hh * HW
                        w1 = min(W, w0 + HW)
                        n = w1 - w0
                        usl = u[:, par * W2 + w0: par * W2 + w1]
                        ssl = scl[:, par * W2 + w0: par * W2 + w1]
                        vector.tensor_scalar(t1[:, 0:n], usl, int(abs(shs)),
                                             int(0x8000), ops, Alu.bitwise_and)
                        vector.tensor_scalar(t2[:, 0:n], usl, int(abs(shm)),
                                             int(0x0E00), opm, Alu.bitwise_and)
                        vector.tensor_tensor(t1[:, 0:n], t1[:, 0:n], t2[:, 0:n],
                                             Alu.bitwise_or)
                        ins = vector.scalar_tensor_tensor(
                            pl[:, ppar * W2 + w0: ppar * W2 + w1],
                            t1.ap()[:, 0:n].bitcast(F16), 16384.0, ssl,
                            Alu.mult, Alu.mult)
                    c.sVd.inc(ins)
        blk.vector(dq)

        def wr(sync):
            for t, (kind, sl) in enumerate(slices):
                W = W2 if kind == "gu" else DM
                for r in range(4):
                    pidx = 4 * t + r
                    ppar = pidx % 2
                    kap = 4 * sl + r
                    sync.wait_ge(c.sVd.h, base_vd + pidx + 1)
                    dst = c.gu_scr[kap] if kind == "gu" else c.dn_scr[kap]
                    c.sWr0.inc(sync.dma_start(
                        out=dst, in_=pl[:, ppar * W2: ppar * W2 + W]), 16)
            sync.wait_ge(c.sWr0.h, c.sWr0.n)
        blk.sync(wr)
    _barrier(c, blk)


def _emit_load_x(c, blk):
    def body(sync):
        c.sIn0.inc(sync.dma_start(
            out=c.xT.ap()[:, :].rearrange("p (tau t) -> p tau t", t=T),
            in_=c.xs.rearrange("(tau p) t -> p tau t", p=128)), 16)
    blk.sync(body)
    tin = c.sIn0.n

    def gate(vector):
        vector.wait_ge(c.sIn0.h, tin)
    blk.vector(gate)


def _emit_reload_x(c, blk):
    def body(scalar):
        c.sSc.inc(scalar.dma_start(out=c.xT[:], in_=c.xspill[:, :]), 16)
    blk.scalar(body)
    tin = c.sSc.n

    def gate(vector):
        vector.wait_ge(c.sSc.h, tin)
    blk.vector(gate)


def _emit_store(c, blk):
    _barrier(c, blk)

    def body(scalar):
        c.sSc.inc(scalar.dma_start(
            out=c.xo.rearrange("(tau p) t -> p tau t", p=128),
            in_=c.xT.ap()[:, :].rearrange("p (tau t) -> p tau t", t=T)), 16)
        scalar.wait_ge(c.sSc.h, c.sSc.n)
    blk.scalar(body)


def _emit_ln(c, blk, col0, W, dest):
    nh = W // 512
    base_spe = c.sPe.n
    base_sve = c.sVe.n

    def vcast(vector):
        for i in range(NT_D * nh):
            tau, h = i // nh, i % nh
            sl2 = 512 * (i % 2)
            if i >= 2:
                vector.wait_ge(c.sPe.h, base_spe + i - 1)
            sl = slice(T * tau + col0 + 512 * h, T * tau + col0 + 512 * (h + 1))
            vector.tensor_copy(c.xb[:, sl2:sl2 + 512], c.xT[:, sl])
            c.sVe.inc(vector.tensor_tensor(
                c.xsq[:, sl2:sl2 + 512], c.xb[:, sl2:sl2 + 512],
                c.xb[:, sl2:sl2 + 512], Alu.mult))
    blk.vector(vcast)

    def pstat(tensor):
        for i in range(NT_D * nh):
            tau, h = i // nh, i % nh
            sl2 = 512 * (i % 2)
            tensor.wait_ge(c.sVe.h, base_sve + i + 1)
            tensor.matmul(c.ps[2 + h][0:1, :], c.onecol_bf_sb[:],
                          c.xb[:, sl2:sl2 + 512],
                          start=(tau == 0), stop=(tau == NT_D - 1))
            c.sPe.inc(tensor.matmul(
                c.ps[4 + h][0:1, :], c.onecol_bf_sb[:], c.xsq[:, sl2:sl2 + 512],
                start=(tau == 0), stop=(tau == NT_D - 1)))
    blk.tensor(pstat)
    tpe = c.sPe.n
    if _LN_MODE < 2:
        return

    def vrows(vector):
        vector.wait_ge(c.sPe.h, tpe)
        for h in range(nh):
            sl = slice(512 * h, 512 * (h + 1))
            vector.tensor_scalar(c.rowA[0:1, sl], c.ps[2 + h][0:1, :], 1.0 / DM,
                                 None, Alu.mult)
            vector.tensor_scalar(c.rowB[0:1, sl], c.ps[4 + h][0:1, :], 1.0 / DM,
                                 None, Alu.mult)
        vector.tensor_tensor(c.aB[0:1, 0:W], c.rowA[0:1, 0:W], c.rowA[0:1, 0:W], Alu.mult)
        vector.tensor_tensor(c.rowB[0:1, 0:W], c.rowB[0:1, 0:W], c.aB[0:1, 0:W], Alu.subtract)
        vector.tensor_scalar(c.rowB[0:1, 0:W], c.rowB[0:1, 0:W], EPS_LN, None, Alu.add)
        c.sVe.inc(vector.tensor_scalar(
            c.rowA[0:1, 0:W], c.rowA[0:1, 0:W], -1.0, None, Alu.mult))
    blk.vector(vrows)
    tv = c.sVe.n

    def srows(scalar, tv=tv):
        scalar.wait_ge(c.sVe.h, tv)
        scalar.activation(c.bB[0:1, 0:W], c.rowB[0:1, 0:W], Act.Ln)
        c.sAc.inc(scalar.activation(c.rowB[0:1, 0:W], c.bB[0:1, 0:W], Act.Exp,
                                    scale=-0.5))
    blk.scalar(srows)
    ta = c.sAc.n
    if _LN_MODE < 3:
        return

    def pbc(tensor, ta=ta):
        tensor.wait_ge(c.sAc.h, ta)
        for h in range(nh):
            sl = slice(512 * h, 512 * (h + 1))
            tensor.matmul(c.ps[2 + h][:, :], c.onerow_32_sb[:], c.rowA[0:1, sl],
                          start=True, stop=True)
            ins = tensor.matmul(c.ps[4 + h][:, :], c.onerow_32_sb[:], c.rowB[0:1, sl],
                                start=True, stop=True)
        c.sPe.inc(ins)
    blk.tensor(pbc)
    tpe2 = c.sPe.n
    if _LN_MODE < 4:
        return

    def sevac(vector, tpe2=tpe2):
        vector.wait_ge(c.sPe.h, tpe2)
        for h in range(nh):
            sl = slice(512 * h, 512 * (h + 1))
            vector.tensor_copy(c.bB[:, sl], c.ps[2 + h][:, :])
            ins = vector.tensor_copy(c.aB[:, sl], c.ps[4 + h][:, :])
        c.sAc.inc(ins)
    blk.vector(sevac)
    ta2 = c.sAc.n
    if _LN_MODE < 5:
        return

    def vapply(vector, ta2=ta2):
        vector.wait_ge(c.sAc.h, ta2)
        for tau in range(NT_D):
            xsl = c.xT[:, T * tau + col0: T * tau + col0 + W]
            vector.tensor_tensor(dest[:, tau, 0:W], xsl, c.bB[:, 0:W], Alu.add)
            ins = vector.tensor_tensor(dest[:, tau, 0:W], dest[:, tau, 0:W],
                                       c.aB[:, 0:W], Alu.mult)
        c.sVe.inc(ins)
    blk.vector(vapply)



def _emit_ln2_hbm(c, blk, dest):
    tspill = c.sSc.n
    base_i = c.sIn0.n
    base_vn = c.sVn.n

    def dmas(sync):
        sync.wait_ge(c.sSc.h, tspill)
        for tau in range(NT_D):
            par = tau % 2
            if tau >= 2:
                sync.wait_ge(c.sVn.h, base_vn + 2 * (tau - 1))
            c.sIn0.inc(sync.dma_start(out=c.xstg[:, 1024 * par:1024 * (par + 1)],
                                      in_=c.xspill[:, T * tau:T * (tau + 1)]), 16)
    blk.sync(dmas)

    base_spe2 = c.sPe.n

    def vcast(vector):
        for i in range(NT_D * 2):
            tau, h = i // 2, i % 2
            par = tau % 2
            sl2 = 512 * (i % 2)
            vector.wait_ge(c.sIn0.h, base_i + 16 * (tau + 1))
            if i >= 2:
                vector.wait_ge(c.sPe.h, base_spe2 + i - 1)
            sl = c.xstg[:, 1024 * par + 512 * h: 1024 * par + 512 * (h + 1)]
            vector.tensor_copy(c.xb[:, sl2:sl2 + 512], sl)
            c.sVn.inc(vector.tensor_tensor(c.xsq[:, sl2:sl2 + 512],
                                           c.xb[:, sl2:sl2 + 512],
                                           c.xb[:, sl2:sl2 + 512], Alu.mult))
    blk.vector(vcast)

    def pstat(tensor):
        for i in range(NT_D * 2):
            tau, h = i // 2, i % 2
            sl2 = 512 * (i % 2)
            tensor.wait_ge(c.sVn.h, base_vn + i + 1)
            tensor.matmul(c.ps[2 + h][0:1, :], c.onecol_bf_sb[:],
                          c.xb[:, sl2:sl2 + 512],
                          start=(tau == 0), stop=(tau == NT_D - 1))
            c.sPe.inc(tensor.matmul(
                c.ps[4 + h][0:1, :], c.onecol_bf_sb[:], c.xsq[:, sl2:sl2 + 512],
                start=(tau == 0), stop=(tau == NT_D - 1)))
    blk.tensor(pstat)
    tpe = c.sPe.n
    W = 1024

    def vrows(vector):
        vector.wait_ge(c.sPe.h, tpe)
        for h in range(2):
            sl = slice(512 * h, 512 * (h + 1))
            vector.tensor_scalar(c.rowA[0:1, sl], c.ps[2 + h][0:1, :], 1.0 / DM,
                                 None, Alu.mult)
            vector.tensor_scalar(c.rowB[0:1, sl], c.ps[4 + h][0:1, :], 1.0 / DM,
                                 None, Alu.mult)
        vector.tensor_tensor(c.aB[0:1, 0:W], c.rowA[0:1, 0:W], c.rowA[0:1, 0:W], Alu.mult)
        vector.tensor_tensor(c.rowB[0:1, 0:W], c.rowB[0:1, 0:W], c.aB[0:1, 0:W], Alu.subtract)
        vector.tensor_scalar(c.rowB[0:1, 0:W], c.rowB[0:1, 0:W], EPS_LN, None, Alu.add)
        c.sVe.inc(vector.tensor_scalar(
            c.rowA[0:1, 0:W], c.rowA[0:1, 0:W], -1.0, None, Alu.mult))
    blk.vector(vrows)
    tv = c.sVe.n

    def srows(scalar, tv=tv):
        scalar.wait_ge(c.sVe.h, tv)
        scalar.activation(c.bB[0:1, 0:W], c.rowB[0:1, 0:W], Act.Ln)
        c.sAc.inc(scalar.activation(c.rowB[0:1, 0:W], c.bB[0:1, 0:W], Act.Exp,
                                    scale=-0.5))
    blk.scalar(srows)
    ta = c.sAc.n

    def pbc(tensor, ta=ta):
        tensor.wait_ge(c.sAc.h, ta)
        for h in range(2):
            sl = slice(512 * h, 512 * (h + 1))
            tensor.matmul(c.ps[2 + h][:, :], c.onerow_32_sb[:], c.rowA[0:1, sl],
                          start=True, stop=True)
            ins = tensor.matmul(c.ps[4 + h][:, :], c.onerow_32_sb[:], c.rowB[0:1, sl],
                                start=True, stop=True)
        c.sPe.inc(ins)
    blk.tensor(pbc)
    tpe2 = c.sPe.n

    def sevac(scalar, tpe2=tpe2):
        scalar.wait_ge(c.sPe.h, tpe2)
        for h in range(2):
            sl = slice(512 * h, 512 * (h + 1))
            scalar.copy(c.bB[:, sl], c.ps[2 + h][:, :])
            ins = scalar.copy(c.aB[:, sl], c.ps[4 + h][:, :])
        c.sAc.inc(ins)
    blk.scalar(sevac)
    ta2 = c.sAc.n

    base_i2 = c.sIn0.n
    base_va = c.sVn.n

    def dmas2(sync):
        sync.wait_ge(c.sVn.h, base_vn + 2 * NT_D)
        for tau in range(NT_D):
            par = tau % 2
            if tau >= 2:
                sync.wait_ge(c.sVn.h, base_va + (tau - 1))
            c.sIn0.inc(sync.dma_start(out=c.xstg[:, 1024 * par:1024 * (par + 1)],
                                      in_=c.xspill[:, T * tau:T * (tau + 1)]), 16)
    blk.sync(dmas2)

    def vapply(vector):
        vector.wait_ge(c.sAc.h, ta2)
        for tau in range(NT_D):
            par = tau % 2
            vector.wait_ge(c.sIn0.h, base_i2 + 16 * (tau + 1))
            xsl = c.xstg[:, 1024 * par:1024 * (par + 1)]
            vector.tensor_tensor(dest[:, tau, 0:1024], xsl, c.bB[:, 0:1024], Alu.add)
            c.sVn.inc(vector.tensor_tensor(dest[:, tau, 0:1024], dest[:, tau, 0:1024],
                                           c.aB[:, 0:1024], Alu.mult))
    blk.vector(vapply)


def _emit_ffn(c, blk, nc, tag):
    with nc.sbuf_tensor(f"hT{tag}", [128, NT_I, 512], F16) as hT, \
         nc.sbuf_tensor(f"wbf{tag}", [128, 2 * 16 * 512], F16) as wbf, \
         nc.sbuf_tensor(f"ub{tag}", [128, NT_D * 512], F32) as ubuf:
        lnb = ubuf.ap()[:, :].bitcast(F16)[:, 0:NT_D * 512].rearrange(
            "p (k f) -> p k f", f=512)
        yb = ubuf.ap()[:, :].rearrange("p (k f) -> p k f", f=512)
        wb = [wbf.ap()[:, 0:8192].rearrange("p (k f) -> p k f", k=16),
              wbf.ap()[:, 8192:16384].rearrange("p (k f) -> p k f", k=16)]
        dnb = [wbf.ap()[:, 0:NT_I * 128].rearrange("p (k f) -> p k f", k=NT_I),
               wbf.ap()[:, 8192:8192 + NT_I * 128].rearrange("p (k f) -> p k f", k=NT_I)]
        for sl in range(_FFN_SLICES):
            col0 = 512 * sl
            if _FFN_MODE >= 1:
                _emit_ln(c, blk, col0, 512, lnb)
            if _FFN_MODE >= 2:
                _emit_ffn1(c, blk, wb, lnb, hT)
            if _FFN_MODE >= 3:
                _emit_ffn2(c, blk, dnb, hT, yb, col0)
                _barrier(c, blk)


def _emit_ffn1(c, blk, wb, lnb, hT):
    base_i = [c.sIn0.n, c.sIn1.n]
    base_pp, base_as, base_vh = c.sPp.n, c.sAs.n, c.sVh.n
    t_lnb = c.sVe.n
    t_sev = c.sAc.n

    def dmas(sync):
        for w in range(GU_W):
            par = w % 2
            sem = [c.sIn0, c.sIn1][par]
            if w >= 2:
                sync.wait_ge(c.sPp.h, base_pp + 2 * (w - 1))
            sem.inc(sync.dma_start(out=wb[par][:, :, :],
                                   in_=c.gu_scr[:, :, 512 * w:512 * (w + 1)]
                                   .rearrange("k p f -> p k f")), 16)
    blk.sync(dmas)

    def mms(tensor):
        tensor.wait_ge(c.sVe.h, t_lnb)
        tensor.wait_ge(c.sAc.h, t_sev)
        for w in range(GU_W):
            par = w % 2
            sem = [c.sIn0, c.sIn1][par]
            tensor.wait_ge(sem.h, base_i[par] + 16 * (w // 2 + 1))
            for i in range(4):
                p = 2 * w + i // 2
                bankset = p % 2
                dst = c.ps[0 + bankset] if i % 2 == 0 else c.ps[2 + bankset]
                if i % 2 == 0 and p >= 2:
                    tensor.wait_ge(c.sAs.h, base_as + p - 1)
                    tensor.wait_ge(c.sVh.h, base_vh + p - 1)
                for k in range(NT_D):
                    ins = tensor.matmul(dst[:, :], wb[par][:, k, 128 * i:128 * (i + 1)],
                                        lnb[:, k, :], start=(k == 0), stop=(k == NT_D - 1))
                if i % 2 == 1:
                    c.sPp.inc(ins)
    blk.tensor(mms)

    def silu_s(scalar):
        for p in range(2 * GU_W):
            bankset = p % 2
            scalar.wait_ge(c.sPp.h, base_pp + p + 1)
            if p >= 2:
                scalar.wait_ge(c.sVh.h, base_vh + p - 1)
            c.sAs.inc(scalar.activation(c.hsil[:, 512 * bankset:512 * (bankset + 1)],
                                        c.ps[0 + bankset][:, :], Act.Silu))
    blk.scalar(silu_s)

    def hmul(vector):
        for p in range(2 * GU_W):
            bankset = p % 2
            vector.wait_ge(c.sAs.h, base_as + p + 1)
            c.sVh.inc(vector.scalar_tensor_tensor(
                hT[:, p, :], c.ps[2 + bankset][:, :], H_SCALE,
                c.hsil[:, 512 * bankset:512 * (bankset + 1)], Alu.mult, Alu.mult))
    blk.vector(hmul)


def _emit_ffn2(c, blk, dnb, hT, yb, col0):
    base_i = [c.sIn0.n, c.sIn1.n]
    base_py, base_ay, base_vq = c.sPy.n, c.sAy.n, c.sVq.n
    base_pe = c.sPe.n
    t_pp = c.sPp.n
    t_vh = c.sVh.n

    def dmas(sync):
        sync.wait_ge(c.sPp.h, t_pp)
        for m in range(NT_D):
            par = m % 2
            sem = [c.sIn0, c.sIn1][par]
            if m >= 2:
                sync.wait_ge(c.sPy.h, base_py + m - 1)
            sem.inc(sync.dma_start(out=dnb[par][:, :, :],
                                   in_=c.dn_scr[:, :, 128 * m:128 * (m + 1)]
                                   .rearrange("k p f -> p k f")), 16)
    blk.sync(dmas)

    def mms(tensor):
        tensor.wait_ge(c.sVh.h, t_vh)
        for m in range(NT_D):
            par = m % 2
            sem = [c.sIn0, c.sIn1][par]
            tensor.wait_ge(sem.h, base_i[par] + 16 * (m // 2 + 1))
            if m >= 2:
                tensor.wait_ge(c.sAy.h, base_ay + m - 1)
            dst = c.ps[4 + par]
            for k in range(NT_I):
                ins = tensor.matmul(dst[:, :], dnb[par][:, k, :], hT[:, k, :],
                                    start=(k == 0), stop=(k == NT_I - 1))
            c.sPy.inc(ins)
    blk.tensor(mms)

    def yevac(scalar):
        for m in range(NT_D):
            par = m % 2
            scalar.wait_ge(c.sPy.h, base_py + m + 1)
            c.sAy.inc(scalar.copy(yb[:, m, :], c.ps[4 + par][:, :]))
    blk.scalar(yevac)

    def vsq(vector):
        for m in range(NT_D):
            vector.wait_ge(c.sAy.h, base_ay + m + 1)
            if m >= 2:
                vector.wait_ge(c.sPe.h, base_pe + m - 1)
            c.sVq.inc(vector.tensor_tensor(
                c.ysq[:, 512 * (m % 2):512 * (m % 2 + 1)], yb[:, m, :],
                yb[:, m, :], Alu.mult))
    blk.vector(vsq)

    def pstat(tensor):
        for m in range(NT_D):
            tensor.wait_ge(c.sVq.h, base_vq + m + 1)
            c.sPe.inc(tensor.matmul(
                c.ps[2][0:1, :], c.onecol_bf_sb[:],
                c.ysq[:, 512 * (m % 2):512 * (m % 2 + 1)],
                start=(m == 0), stop=(m == NT_D - 1)))
    blk.tensor(pstat)
    tstat = c.sPe.n

    def vrow(vector, tstat=tstat):
        vector.wait_ge(c.sPe.h, tstat)
        c.sVe.inc(vector.tensor_scalar(c.rowA[0:1, 0:512], c.ps[2][0:1, :],
                                       1.0 / DM, EPS_RMS, Alu.mult, Alu.add))
    blk.vector(vrow)
    tv = c.sVe.n

    def srow(scalar, tv=tv):
        scalar.wait_ge(c.sVe.h, tv)
        scalar.activation(c.rowB[0:1, 0:512], c.rowA[0:1, 0:512], Act.Ln)
        scalar.activation(c.rowA[0:1, 0:512], c.rowB[0:1, 0:512], Act.Exp, scale=-0.5)
        c.sAc.inc(scalar.mul(c.rowA[0:1, 0:512], c.rowA[0:1, 0:512], 0.5))
    blk.scalar(srow)
    ta = c.sAc.n

    def pbc(tensor, ta=ta):
        tensor.wait_ge(c.sAc.h, ta)
        c.sPe.inc(tensor.matmul(c.ps[3][:, :], c.onerow_32_sb[:], c.rowA[0:1, 0:512],
                                start=True, stop=True))
    blk.tensor(pbc)
    tb = c.sPe.n

    def sevac(scalar, tb=tb):
        scalar.wait_ge(c.sPe.h, tb)
        c.sAc.inc(scalar.copy(c.aB[:, 0:512], c.ps[3][:, :]))
    blk.scalar(sevac)
    ta2 = c.sAc.n

    def vadd(vector, ta2=ta2):
        vector.wait_ge(c.sAc.h, ta2)
        for m in range(NT_D):
            vector.tensor_tensor(c.tmpv[:, 0:512], yb[:, m, :], c.aB[:, 0:512], Alu.mult)
            xsl = c.xT[:, T * m + col0: T * m + col0 + 512]
            ins = vector.tensor_tensor(xsl, xsl, c.tmpv[:, 0:512], Alu.add)
        c.sVe.inc(ins)
    blk.vector(vadd)


def _emit_attn_pre(c, blk, nc):
    ln2 = c.reg_ln2.enter_context(nc.sbuf_tensor("ln2", [128, NT_D, 1024], F16))

    def dprld(sync):
        c.sIn0.inc(sync.dma_start(out=c.dpr[:], in_=c.dpr_scr[:, 0:2 * DW]), 16)
    blk.sync(dprld)

    _emit_ln2_hbm(c, blk, ln2.ap())
    _barrier(c, blk)
    if _DBG:
        def dmpl(sync):
            c.sIn0.inc(sync.dma_start(out=c.dbg_ln2[:, :],
                                      in_=ln2.ap()[:, :, :].rearrange("p k f -> p (k f)")), 16)
            sync.wait_ge(c.sIn0.h, c.sIn0.n)
        blk.sync(dmpl)
        _barrier(c, blk)

    with nc.sbuf_tensor("KTl", [128, 4 * 1024], F16) as KTl, \
         nc.sbuf_tensor("VTl", [128, 4 * 1024], F16) as VTl, \
         nc.sbuf_tensor("Vtok", [128, 8 * 512], BF16) as Vtok, \
         nc.sbuf_tensor("wraw", [128, DM], F32) as wraw, \
         nc.sbuf_tensor("wf16", [128, DM], F16) as wf16, \
         nc.sbuf_tensor("wTt", [128, 16 * 128], F16) as wT:

        def proj(wsrc, nm, out_cb):
            base_i = c.sIn0.n
            base_ac, base_pt, base_vt = c.sAc.n, c.sPt.n, c.sVt.n
            base_pe, base_ve = c.sPe.n, c.sVe.n

            def dmas(sync):
                for m in range(nm):
                    sync.wait_ge(c.sAc.h, base_ac + m)
                    c.sIn0.inc(sync.dma_start(out=wraw[:, :],
                                              in_=wsrc[128 * m:128 * (m + 1), :]), 16)
            blk.sync(dmas)

            def casts(scalar):
                for m in range(nm):
                    scalar.wait_ge(c.sIn0.h, base_i + 16 * (m + 1))
                    scalar.wait_ge(c.sPt.h, base_pt + 16 * m)
                    c.sAc.inc(scalar.copy(wf16[:, :], wraw[:, :]))
            blk.scalar(casts)

            def transp(tensor):
                for m in range(nm):
                    tensor.wait_ge(c.sAc.h, base_ac + m + 1)
                    for k in range(16):
                        idx = m * 16 + k
                        tensor.wait_ge(c.sVt.h, base_vt + max(idx - 1, 0))
                        c.sPt.inc(tensor.transpose(
                            c.pt[idx % 2][:, 0:128],
                            wf16[:, 128 * k: 128 * (k + 1)],
                            c.ident_sb[:]))
                    tensor.wait_ge(c.sVt.h, base_vt + 16 * (m + 1))
                    tensor.wait_ge(c.sVe.h, base_ve + max(m - 1, 0))
                    for half in range(2):
                        for k in range(16):
                            ins = tensor.matmul(
                                c.ps[2 * (m % 2) + half][:, :], wT[:, 128 * k:128 * (k + 1)],
                                ln2[:, k, 512 * half:512 * (half + 1)],
                                start=(k == 0), stop=(k == 15))
                    c.sPe.inc(ins)
            blk.tensor(transp)

            def vec(vector):
                def oevac(m):
                    vector.wait_ge(c.sPe.h, base_pe + m + 1)
                    ins = out_cb(vector, m)
                    c.sVe.inc(ins)
                for m in range(nm):
                    for k in range(16):
                        idx = m * 16 + k
                        vector.wait_ge(c.sPt.h, base_pt + idx + 1)
                        c.sVt.inc(vector.tensor_copy(wT[:, 128 * k:128 * (k + 1)],
                                                     c.pt[idx % 2][:, 0:128]))
                    if m >= 1:
                        oevac(m - 1)
                oevac(nm - 1)
            blk.vector(vec)

        def q_out(vector, m):
            for half in range(2):
                ins = vector.tensor_scalar(
                    c.QT[:, 1024 * m + 512 * half: 1024 * m + 512 * (half + 1)],
                    c.ps[2 * (m % 2) + half][:, :], SCALE_QK, None, Alu.mult)
            return ins

        def k_out(vector, m):
            for half in range(2):
                ins = vector.tensor_copy(
                    KTl[:, 1024 * m + 512 * half: 1024 * m + 512 * (half + 1)],
                    c.ps[2 * (m % 2) + half][:, :])
            return ins

        def v_out(vector, m):
            for half in range(2):
                ins = vector.tensor_copy(
                    VTl[:, 1024 * m + 512 * half: 1024 * m + 512 * (half + 1)],
                    c.ps[2 * (m % 2) + half][:, :])
            return ins

        proj(c.wqd, 16, q_out)
        proj(c.wkd, 4, k_out)
        proj(c.wvd, 4, v_out)

        base_pt, base_vt = c.sPt.n, c.sVt.n

        def vtr(tensor):
            i = 0
            for g in range(4):
                for t in range(8):
                    tensor.wait_ge(c.sVt.h, base_vt + max(i - 1, 0))
                    c.sPt.inc(tensor.transpose(
                        c.pt[i % 2][:, 0:128],
                        VTl[:, 1024 * g + 128 * t: 1024 * g + 128 * (t + 1)],
                        c.ident_sb[:]))
                    i += 1
        blk.tensor(vtr)

        def vtre(vector):
            i = 0
            for g in range(4):
                for t in range(8):
                    vector.wait_ge(c.sPt.h, base_pt + i + 1)
                    c.sVt.inc(vector.tensor_copy(
                        Vtok[:, 512 * t + 128 * g: 512 * t + 128 * (g + 1)],
                        c.pt[i % 2][:, 0:128]))
                    i += 1
        blk.vector(vtre)
        tvt = c.sVt.n

        def stage(scalar, tvt=tvt):
            scalar.wait_ge(c.sVt.h, tvt)
            c.sSc.inc(scalar.dma_start(out=c.cc_in[0],
                                       in_=KTl.ap()[:, :].bitcast(U16)), 16)
            c.sSc.inc(scalar.dma_start(out=c.cc_in[1],
                                       in_=Vtok.ap()[:, :].bitcast(U16)), 16)
        blk.scalar(stage)
        tst = c.sSc.n

        def coll(gpsimd, tst=tst):
            gpsimd.wait_ge(c.sSc.h, tst)
            c.sCc.inc(gpsimd.collective_compute(
                "AllGather", Alu.bypass,
                replica_groups=[[0, 1, 2, 3], [4, 5, 6, 7]],
                ins=[c.cc_in[:, :, :]], outs=[c.cc_out[:, :, :, :]]))
        blk.gpsimd(coll)

    c.reg_ln2.close()
    Ek = c.reg_kv.enter_context
    c.KT = Ek(nc.sbuf_tensor("KT", [128, 4 * 4096], F16))
    c.Vsb = Ek(nc.sbuf_tensor("Vsb", [128, 32 * 512], BF16))
    c.dpr2 = Ek(nc.sbuf_tensor("dpr2", [128, 1024], F16))
    c.cmax = Ek(nc.sbuf_tensor("cmax", [128, 2 * NH], F16))
    c.denr = Ek(nc.sbuf_tensor("denr", [1, 512], F32))
    c.oraw = Ek(nc.sbuf_tensor("oraw", [128, 2 * 512], F16))
    tcc = c.sCc.n

    def gload(scalar, tcc=tcc):
        scalar.wait_ge(c.sCc.h, tcc)
        c.sSc.inc(scalar.dma_start(out=c.dpr2[:],
                                   in_=c.dpr_scr[:, 2 * DW:2 * DW + 1024]), 16)
        for rk in range(4):
            for g in range(4):
                c.sSc.inc(scalar.dma_start(
                    out=c.KT.ap()[:, 4096 * g + 1024 * rk: 4096 * g + 1024 * (rk + 1)]
                    .bitcast(U16),
                    in_=c.cc_out[rk, 0][:, 1024 * g:1024 * (g + 1)]), 16)
            c.sSc.inc(scalar.dma_start(
                out=c.Vsb.ap()[:, 4096 * rk: 4096 * (rk + 1)].bitcast(U16),
                in_=c.cc_out[rk, 1][:, :]), 16)
    blk.scalar(gload)
    tg = c.sSc.n

    base_pn, base_vn = c.sPn.n, c.sVn.n

    def ksamp(vector, tg=tg):
        vector.wait_ge(c.sSc.h, tg)
        ins = None
        for g in range(4):
            src = c.KT.ap()[:, 4096 * g:4096 * (g + 1)].rearrange(
                "p (j s) -> p s j", s=32)[:, 0, :]
            ins = vector.tensor_copy(c.qsq[:, 128 * g:128 * (g + 1)], src)
        c.sVn.inc(ins)
    blk.vector(ksamp)

    tqv = c.sVe.n

    def prep_mm(tensor):
        tensor.wait_ge(c.sVe.h, tqv)
        for hc in range(2 * NH):
            h, ch = hc // 2, hc % 2
            g = h // 4
            tensor.wait_ge(c.sVn.h, base_vn + 1)
            if hc >= 1:
                tensor.wait_ge(c.sVn.h, base_vn + 1 + hc)
            for qs in range(4):
                ins = tensor.matmul(
                    c.ps[2][:, 128 * qs:128 * (qs + 1)],
                    c.QT[:, 1024 * h + 512 * ch + 128 * qs: 1024 * h + 512 * ch + 128 * (qs + 1)],
                    c.qsq[:, 128 * g:128 * (g + 1)], start=True, stop=True)
            c.sPn.inc(ins)
    blk.tensor(prep_mm)

    def prep_v(vector):
        for hc in range(2 * NH):
            h, ch = hc // 2, hc % 2
            vector.wait_ge(c.sPn.h, base_pn + hc + 1)
            for qs in range(4):
                vector.scalar_tensor_tensor(
                    c.smb[:, 512 + 128 * qs:512 + 128 * (qs + 1)],
                    c.dpr2[:, 128 * (4 * ch + qs):128 * (4 * ch + qs + 1)],
                    -SLOPES[h], c.ps[2][:, 128 * qs:128 * (qs + 1)],
                    Alu.mult, Alu.add)
            ins = vector.reduce_max(c.cmax[:, hc:hc + 1], c.smb[:, 512:1024], axis=AX)
            c.sVn.inc(ins)
    blk.vector(prep_v)
    tnv = c.sVn.n

    def redt(tensor, tnv=tnv):
        tensor.wait_ge(c.sVn.h, tnv)
        c.sPn.inc(tensor.transpose(c.pt[0][0:32, 0:128],
                                   c.cmax[:, :], c.ident_sb[:]))
    blk.tensor(redt)
    tnp = c.sPn.n

    def redv(vector, tnp=tnp):
        vector.wait_ge(c.sPn.h, tnp)
        c.sVn.inc(vector.reduce_max(c.cmax[0:32, 0:1],
                                    c.pt[0][0:32, 0:128], axis=AX))
    blk.vector(redv)
    tnv2 = c.sVn.n

    def redt2(tensor, tnv2=tnv2):
        tensor.wait_ge(c.sVn.h, tnv2)
        c.sPn.inc(tensor.transpose(c.pt[1][0:1, 0:128], c.cmax[:, 0:1], c.ident_sb[:]))
    blk.tensor(redt2)
    tnp2 = c.sPn.n

    def redv2(vector, tnp2=tnp2):
        vector.wait_ge(c.sPn.h, tnp2)
        c.sVn.inc(vector.tensor_scalar(c.rowC[0:1, 0:32], c.pt[1][0:1, 0:32],
                                       -1.0, None, Alu.mult))
    blk.vector(redv2)
    tnv3 = c.sVn.n

    def bc(tensor, tnv3=tnv3):
        tensor.wait_ge(c.sVn.h, tnv3)
        c.sPe.inc(tensor.matmul(c.ps[2][:, 0:32], c.onerow_32_sb[:],
                                c.rowC[0:1, 0:32], start=True, stop=True))
    blk.tensor(bc)
    tp = c.sPe.n

    def ev(scalar, tp=tp):
        scalar.wait_ge(c.sPe.h, tp)
        c.sAc.inc(scalar.copy(c.negM[:, :], c.ps[2][:, 0:32]))
    blk.scalar(ev)
    _barrier(c, blk)


def _emit_attn_main(c, blk):
    M = 2 * NH
    NJ = NKT
    base_ps, base_po, base_pr = c.sPs.n, c.sPo.n, c.sPr.n
    base_vs, base_vo, base_vm = c.sVs.n, c.sVo.n, c.sVm.n
    base_ae, base_ar = c.sAe.n, c.sAr.n
    base_q, base_dc = c.sPw0.n, c.sPw1.n
    RBJ = 6

    def rbmm(tensor, m):
        tensor.wait_ge(c.sPw0.h, base_q + m + 1)
        tensor.wait_ge(c.sAr.h, base_ar + m)
        c.sPr.inc(tensor.matmul(c.ps[5][:, :], c.onerow_32_sb[:],
                                c.rowB[0:1, 0:512], start=True, stop=True))

    def pe(tensor):
        for m in range(M):
            h, ch = m // 2, m % 2
            g = h // 4
            qsl = slice(1024 * h + 512 * ch, 1024 * h + 512 * (ch + 1))
            po = c.ps[2 + m % 2]
            for j in range(NJ + 3):
                gidx = m * NJ + j
                if j < NJ:
                    if gidx >= 2:
                        tensor.wait_ge(c.sVs.h, base_vs + gidx - 1)
                    c.sPs.inc(tensor.matmul(
                        c.ps[j % 2][:, :],
                        c.KT[:, 4096 * g + 128 * j: 4096 * g + 128 * (j + 1)],
                        c.QT[:, qsl], start=True, stop=True))
                if j == RBJ and m >= 1:
                    rbmm(tensor, m - 1)
                if j >= 3:
                    jj = j - 3
                    gj = m * NJ + jj
                    tensor.wait_ge(c.sAe.h, base_ae + gj + 1)
                    if jj == 0 and m >= 2:
                        tensor.wait_ge(c.sVm.h, base_vm + m - 1)
                    esl = c.Eb[:, 512 * (gj % 4):512 * (gj % 4 + 1)]
                    tensor.matmul(
                        po[:, :],
                        c.Vsb[:, 512 * jj + 128 * g: 512 * jj + 128 * (g + 1)],
                        esl, start=(jj == 0), stop=(jj == NJ - 1))
                    if jj == 0 and m >= 1:
                        tensor.wait_ge(c.sPw1.h, base_dc + m)
                    c.sPo.inc(tensor.matmul(
                        c.ps[4][0:1, :], c.onecol_bf_sb[:], esl,
                        start=(jj == 0), stop=(jj == NJ - 1)))

        rbmm(tensor, M - 1)
    blk.tensor(pe)

    def boundary(scalar, mp):
        scalar.wait_ge(c.sPo.h, base_po + 32 * (mp + 1))
        c.sPw1.inc(scalar.copy(c.denr[0:1, :], c.ps[4][0:1, :]))
        yield
        c.sVm.inc(scalar.copy(c.oraw[:, 512 * (mp % 2):512 * (mp % 2 + 1)],
                              c.ps[2 + mp % 2][:, :]))
        yield
        scalar.activation(c.rowA[0:1, 0:512], c.denr[0:1, :], Act.Ln)
        yield
        c.sPw0.inc(scalar.activation(c.rowB[0:1, 0:512], c.rowA[0:1, 0:512],
                                     Act.Exp, scale=-1.0))
        yield
        scalar.wait_ge(c.sPr.h, base_pr + mp + 1)
        if mp >= 1:
            scalar.wait_ge(c.sVo.h, base_vo + mp)
        c.sAr.inc(scalar.copy(c.aB[:, 0:512], c.ps[5][:, :]))

    def sc(scalar):
        bnd = None
        for m in range(M):
            h = m // 2
            if m >= 1:
                bnd = boundary(scalar, m - 1)
                next(bnd)
            for j in range(NJ):
                gidx = m * NJ + j
                scalar.wait_ge(c.sVs.h, base_vs + gidx + 1)
                if gidx >= 4:
                    scalar.wait_ge(c.sPo.h, base_po + gidx - 3)
                c.sAe.inc(scalar.activation(
                    c.Eb[:, 512 * (gidx % 4):512 * (gidx % 4 + 1)],
                    c.smb[:, 512 * (gidx % 4):512 * (gidx % 4 + 1)],
                    Act.Exp, bias=c.negM[:, m:m + 1]))
                if bnd is not None and j in (0, 1, 2, 7):
                    try:
                        next(bnd)
                    except StopIteration:
                        bnd = None
        bnd = boundary(scalar, M - 1)
        for _ in range(5):
            try:
                next(bnd)
            except StopIteration:
                break
    blk.scalar(sc)

    def ve(vector):
        omult = [None]

        def do_omult(mp):
            vector.wait_ge(c.sAr.h, base_ar + mp + 1)
            hp, chp = mp // 2, mp % 2
            c.sVo.inc(vector.tensor_tensor(
                c.oT[:, 1024 * hp + 512 * chp: 1024 * hp + 512 * (chp + 1)],
                c.oraw[:, 512 * (mp % 2):512 * (mp % 2 + 1)],
                c.aB[:, 0:512], Alu.mult))
        for m in range(M):
            h, ch = m // 2, m % 2
            for j in range(NJ):
                gidx = m * NJ + j
                vector.wait_ge(c.sPs.h, base_ps + gidx + 1)
                if gidx >= 4:
                    vector.wait_ge(c.sAe.h, base_ae + gidx - 3)
                dsl = c.dpr[:, DW * ch + 128 * (31 - j): DW * ch + 128 * (31 - j) + 512]
                c.sVs.inc(vector.scalar_tensor_tensor(
                    c.smb[:, 512 * (gidx % 4):512 * (gidx % 4 + 1)], dsl, -SLOPES[h],
                    c.ps[j % 2][:, :], Alu.mult, Alu.add))
                if j == 20 and m >= 1:
                    do_omult(m - 1)
        do_omult(M - 1)
    blk.vector(ve)
    _barrier(c, blk)


def _emit_wo(c, blk, nc):
    with nc.sbuf_tensor("wraw2", [128, 2 * DM], F32) as wraw, \
         nc.sbuf_tensor("wf162", [128, 2 * DM], F16) as wf16, \
         nc.sbuf_tensor("wT2", [128, 16 * 128], F16) as wT:
        base_i = [c.sIn0.n, c.sIn1.n]
        base_ac, base_pt, base_vt = c.sAc.n, c.sPt.n, c.sVt.n
        base_pe, base_ve = c.sPe.n, c.sVe.n

        def dmas(sync):
            for m in range(NT_D):
                par = m % 2
                sem = [c.sIn0, c.sIn1][par]
                if m >= 2:
                    sync.wait_ge(c.sAc.h, base_ac + m - 1)
                sem.inc(sync.dma_start(out=wraw[:, DM * par:DM * (par + 1)],
                                       in_=c.wod[128 * m:128 * (m + 1), :]), 16)
        blk.sync(dmas)

        def casts(scalar):
            for m in range(NT_D):
                par = m % 2
                sem = [c.sIn0, c.sIn1][par]
                scalar.wait_ge(sem.h, base_i[par] + 16 * (m // 2 + 1))
                if m >= 2:
                    scalar.wait_ge(c.sPt.h, base_pt + 16 * (m - 1))
                c.sAc.inc(scalar.copy(wf16[:, DM * par:DM * (par + 1)],
                                      wraw[:, DM * par:DM * (par + 1)]))
        blk.scalar(casts)

        def transp(tensor):
            for m in range(NT_D):
                par = m % 2
                tensor.wait_ge(c.sAc.h, base_ac + m + 1)
                for k in range(16):
                    idx = m * 16 + k
                    if idx >= 2:
                        tensor.wait_ge(c.sVt.h, base_vt + idx - 1)
                    c.sPt.inc(tensor.transpose(
                        c.pt[idx % 2][:, 0:128],
                        wf16[:, DM * par + 128 * k: DM * par + 128 * (k + 1)],
                        c.ident_sb[:]))
                tensor.wait_ge(c.sVt.h, base_vt + 16 * (m + 1))
                if m >= 2:
                    tensor.wait_ge(c.sVe.h, base_ve + m - 1)
                for half in range(2):
                    for k in range(16):
                        ins = tensor.matmul(
                            c.ps[2 * (m % 2) + half][:, :], wT[:, 128 * k:128 * (k + 1)],
                            c.oT[:, 1024 * k + 512 * half: 1024 * k + 512 * (half + 1)],
                            start=(k == 0), stop=(k == 15))
                c.sPe.inc(ins)
        blk.tensor(transp)

        def vec(vector):
            def resid(m):
                vector.wait_ge(c.sPe.h, base_pe + m + 1)
                for half in range(2):
                    xsl = c.xT[:, T * m + 512 * half: T * m + 512 * (half + 1)]
                    ins = vector.tensor_tensor(xsl, xsl, c.ps[2 * (m % 2) + half][:, :], Alu.add)
                c.sVe.inc(ins)
            for m in range(NT_D):
                for k in range(16):
                    idx = m * 16 + k
                    vector.wait_ge(c.sPt.h, base_pt + idx + 1)
                    c.sVt.inc(vector.tensor_copy(wT[:, 128 * k:128 * (k + 1)],
                                                 c.pt[idx % 2][:, 0:128]))
                if m >= 1:
                    resid(m - 1)
            resid(NT_D - 1)
        blk.vector(vec)
    _barrier(c, blk)


def _stage_core(core, x, wq, wk, wv, wo, gupk_i32, gusc, dnpk_i32, dnsc):
    b = core // 4
    t0 = T * (core % 4)
    xs = np.ascontiguousarray(x[b, t0:t0 + T, :][:, PERM_D].T)

    rows = np.empty(2 * I_FFN, dtype=np.int64)
    for kap in range(NT_I):
        sl = PERM_I[128 * kap:128 * (kap + 1)]
        rows[256 * kap:256 * kap + 128] = sl
        rows[256 * kap + 128:256 * kap + 256] = I_FFN + sl

    delt = np.empty((2, DW, 128), dtype=np.float32)
    bigm = np.empty((2, DW, 128), dtype=np.float16)
    pp = np.arange(128)[:, None]
    ff = np.arange(128)[None, :]
    for ch in range(2):
        q0 = t0 + 512 * ch
        for i in range(NDELT):
            dlt = q0 - 128 * (31 - i)
            raw = (dlt + ff - pp).astype(np.float64)
            delt[ch, 128 * i:128 * (i + 1), :] = np.where(raw > 0, raw, 1e-20)
            bigm[ch, 128 * i:128 * (i + 1), :] = np.where(raw < 0, BIG, 0.0)

    dlt2 = np.empty((8 * 128, 128), dtype=np.float32)
    msk2 = np.empty((8 * 128, 128), dtype=np.float16)
    jj = 32 * np.arange(128)[None, :]
    for i in range(8):
        ch, qs = i // 4, i % 4
        qg = t0 + 512 * ch + 128 * qs + np.arange(128)[:, None]
        raw = (qg - jj).astype(np.float64)
        dlt2[128 * i:128 * (i + 1), :] = np.where(raw > 0, raw, 1e-20)
        msk2[128 * i:128 * (i + 1), :] = np.where(raw < 0, BIG, 0.0)

    return {
        "xs": xs, "dlt2": dlt2, "msk2": msk2,
        "wqd": np.ascontiguousarray(wq[:, PERM_D]),
        "wkd": np.ascontiguousarray(wk[:, PERM_D]),
        "wvd": np.ascontiguousarray(wv[:, PERM_D]),
        "wod": np.ascontiguousarray(wo[PERM_D, :]),
        "gupk": np.ascontiguousarray(gupk_i32.view(np.uint16)[rows, :]),
        "gusc": np.ascontiguousarray(gusc[:, rows]),
        "dnpk": np.ascontiguousarray(dnpk_i32.view(np.uint16)[PERM_D, :]),
        "dnsc": np.ascontiguousarray(dnsc[:, PERM_D]),
        "delt": delt, "bigm": bigm,
        "ident": np.eye(128, dtype=np.float16),
        "onecol_bf": np.ones((128, 1), dtype=ml_dtypes.bfloat16),
        "onecol_f16": np.ones((128, 1), dtype=np.float16),
        "onerow_32": np.ones((1, 128), dtype=np.float32),
    }


def kernel(L0_attn_x, L0_attn_Wq, L0_attn_Wk, L0_attn_Wv, L0_attn_Wo,
           L0_ffn_gate_up_packed, L0_ffn_gate_up_scales,
           L0_ffn_down_packed, L0_ffn_down_scales):
    x = np.asarray(L0_attn_x, dtype=np.float32)
    args = (np.asarray(L0_attn_Wq), np.asarray(L0_attn_Wk),
            np.asarray(L0_attn_Wv), np.asarray(L0_attn_Wo),
            np.asarray(L0_ffn_gate_up_packed), np.asarray(L0_ffn_gate_up_scales),
            np.asarray(L0_ffn_down_packed), np.asarray(L0_ffn_down_scales))
    in_maps = [_stage_core(cc, x, *args) for cc in range(NCORE)]
    nc = build_program()
    res = run_bass_kernel_spmd(nc, in_maps, core_ids=list(range(NCORE)))
    out = np.empty((B, S, DM), dtype=np.float32)
    for cc in range(NCORE):
        b = cc // 4
        t0 = T * (cc % 4)
        out[b, t0:t0 + T, PERM_D] = res.results[cc]["xo"]
    return out
